# version 04
# baseline (speedup 1.0000x reference)
import math
import sys
import numpy as np
try:
    import concourse.bass as bass
except ImportError:
    sys.path.insert(0, "/opt/trn_rl_repo")
import concourse.bass as bass
import concourse.mybir as mybir
from concourse.bass_utils import run_bass_kernel_spmd

F32 = mybir.dt.float32
BF16 = mybir.dt.bfloat16
I32 = mybir.dt.int32
AF = mybir.ActivationFunctionType
ALU = mybir.AluOpType
AX = mybir.AxisListType

S, D, H, HD, E, FF = 4096, 1024, 16, 64, 8, 2048
NT = S // 128
NC_ = D // 128
NFC = FF // 128
NQB = S // 512
GRP = 512
NGRP = S // GRP

PI = math.pi
HI2PI = 6.28125
LO2PI = 2 * PI - HI2PI
INV2PI = 1.0 / (2 * PI)


class Sem:
    def __init__(self, nc, name):
        self.h = nc.alloc_semaphore(name)
        self.name = name
        self.n = 0


FENCES = ("p0", "tile", "p15", "exp", "qb")


def build_once(dbg, EVp, final):
    nc = bass.Bass("TRN2", target_bir_lowering=False)
    dbg = set(dbg)

    x_h = nc.dram_tensor("x", [S, D], F32, kind="ExternalInput").ap()
    wq_h = nc.dram_tensor("wq", [D, D], F32, kind="ExternalInput").ap()
    wk_h = nc.dram_tensor("wk", [D, D], F32, kind="ExternalInput").ap()
    wv_h = nc.dram_tensor("wv", [D, D], F32, kind="ExternalInput").ap()
    wo_h = nc.dram_tensor("wo", [D, D], F32, kind="ExternalInput").ap()
    wr_h = nc.dram_tensor("wr", [D, E], F32, kind="ExternalInput").ap()
    wg_h = nc.dram_tensor("wg", [E, D, FF], F32, kind="ExternalInput").ap()
    wu_h = nc.dram_tensor("wu", [E, D, FF], F32, kind="ExternalInput").ap()
    wd_h = nc.dram_tensor("wd", [E, FF, D], F32, kind="ExternalInput").ap()
    invf_h = nc.dram_tensor("invf", [128, 32], F32, kind="ExternalInput").ap()
    out_h = nc.dram_tensor("out", [S, D], F32, kind="ExternalOutput").ap()

    def scratch(name, shape, dtype):
        kind = "ExternalOutput" if name in dbg else "Internal"
        return nc.dram_tensor(name, shape, dtype, kind=kind).ap()

    qT_s = scratch("qT_s", [NT, 128, 1024], BF16)
    kT_s = scratch("kT_s", [NT, 128, 1024], BF16)
    v_s = scratch("v_s", [NT, 128, H * 65], BF16)
    g_s = scratch("g_s", [S, D], F32)
    negm_d = scratch("negm_d", [128], F32)

    cur = [20480]
    off = {}

    def carve(name, shape, dtype, at=None):
        size = {F32: 4, BF16: 2, I32: 4}[dtype]
        n_free = int(np.prod(shape[1:]))
        if at is None:
            at = (cur[0] + 31) & ~31
            cur[0] = at + n_free * size
            assert cur[0] <= 229370, (name, cur[0])
        off[name] = at
        return nc.alloc_sbuf_tensor_at(name, list(shape), dtype, offset=at)

    ident = carve("ident", [128, 128], BF16)
    identf = carve("identf", [128, 128], F32)
    masks = carve("masks", [128, 4 * 512], BF16)
    biasT = carve("biasT", [128, 128], F32)
    w_rout = carve("w_rout", [128, NT * E], F32)
    Mbuf = carve("Mbuf", [16, NT], F32)
    kmaxb = carve("kmaxb", [128, NT], F32)
    invfb = carve("invfb", [128, 32], F32)
    ones_f = carve("ones_f", [1, 128], F32)
    posf = carve("posf", [128, NT], F32)
    posi = carve("posi", [128, NT], I32)
    maxk_col = carve("maxk_col", [16, 1], F32)
    negm16 = carve("negm16", [16, 8], F32)
    negm_fl = carve("negm_fl", [1, 128], F32)
    qn2blk = carve("qn2blk", [16, 8], F32)
    smalls = carve("smalls", [128, 16], F32)
    qn2t = carve("qn2t", [128, 16], F32)
    kn2t = carve("kn2t", [128, 16], F32)
    rsm = carve("rsm", [128, 8], F32)
    rsm2 = carve("rsm2", [128, 8], F32)
    rsm3 = carve("rsm3", [128, 8], F32)
    rsm4 = carve("rsm4", [128, 8], F32)
    kk1 = carve("kk1", [32, 1], F32)
    kk2 = carve("kk2", [1, 1], F32)
    angb = carve("angb", [128, 32], F32)
    ubb = carve("ubb", [128, 32], F32)
    nbb = carve("nbb", [128, 32], F32)
    tb2 = carve("tb2", [128, 32], F32)
    PERM_END = (cur[0] + 31) & ~31

    R1 = PERM_END
    xnT = carve("xnT", [128, NC_ * S], BF16, at=R1)
    kTb = carve("kTb", [128, NT * 1024], BF16, at=R1)
    R2 = R1 + 65536

    c1 = [R2]

    def carve1(name, shape, dtype):
        size = {F32: 4, BF16: 2, I32: 4}[dtype]
        at = (c1[0] + 31) & ~31
        c1[0] = at + int(np.prod(shape[1:])) * size
        assert c1[0] <= 229370, (name, c1[0])
        return carve(name, shape, dtype, at=at)

    tabs = carve1("tabs", [128, 4 * 1024], F32)
    wqT = carve1("wqT", [128, NC_ * 1024], BF16)
    wkT = carve1("wkT", [128, NC_ * 1024], BF16)
    wvT = carve1("wvT", [128, NC_ * 1024], BF16)
    xb = carve1("xb", [128, 2 * 1024], F32)
    sqb = carve1("sqb", [128, 1024], F32)
    xnb = carve1("xnb", [128, 1024], BF16)
    qro = carve1("qro", [128, 1024], BF16)
    kro = carve1("kro", [128, 1024], BF16)
    tmA = carve1("tmA", [128, 512], F32)
    tmB = carve1("tmB", [128, 512], F32)
    vsb = carve1("vsb", [128, H * 65], BF16)
    qTt = carve1("qTt", [128, 1024], BF16)
    kTt = carve1("kTt", [128, 1024], BF16)
    wrsb = carve1("wrsb", [128, NC_ * E], BF16)
    stg_w = carve("stg_w", [128, 8192], BF16, at=off["xb"])

    c2 = [R2]

    def carve2(name, shape, dtype):
        size = {F32: 4, BF16: 2}[dtype]
        at = (c2[0] + 31) & ~31
        c2[0] = at + int(np.prod(shape[1:])) * size
        assert c2[0] <= 229370, (name, c2[0])
        return carve(name, shape, dtype, at=at)

    wgsb = carve2("wgsb", [128, NC_ * FF], BF16)
    wusb = carve2("wusb", [128, NC_ * FF], BF16)
    wdsb = carve2("wdsb", [128, NFC * 1024], BF16)
    hT = carve2("hT", [128, NFC * GRP], BF16)
    hsil = carve2("hsil", [128, GRP], F32)
    stg = carve2("stg", [128, 2 * 1024], F32)

    c3 = [R2]

    def carve3(name, shape, dtype):
        size = {F32: 4, BF16: 2}[dtype]
        at = (c3[0] + 31) & ~31
        c3[0] = at + int(np.prod(shape[1:])) * size
        assert c3[0] <= 229370, (name, c3[0])
        return carve(name, shape, dtype, at=at)

    vsb3 = carve3("vsb3", [128, NT * H * 65], BF16)
    qblk = carve3("qblk", [128, 4 * 1024], BF16)
    woT = carve3("woT", [128, NC_ * 1024], BF16)
    Pt = carve3("Pt", [128, 2 * 1024], BF16)
    ontT = carve3("ontT", [128, 2 * NC_ * 512], BF16)
    rdbc = carve3("rdbc", [64, 512], F32)
    lnrow = carve3("lnrow", [1, 512], F32)
    rdrow = carve3("rdrow", [1, 512], F32)
    gb3 = carve3("gb3", [128, 2 * 1024], F32)
    gb3s = carve3("gb3s", [128, 1024], F32)
    stg_wo = carve("stg_wo", [128, 8192], BF16, at=off["gb3"])

    psA = nc.alloc_psum_tensor("psA", [128, 1024], F32)
    ps2 = nc.alloc_psum_tensor("ps2", [128, 1024], F32)
    ps3 = nc.alloc_psum_tensor("ps3", [128, 1024], F32)
    ps4 = nc.alloc_psum_tensor("ps4", [128, 1024], F32)
    psAb = psA.ap().bitcast(BF16)
    ps0b = psAb[:, 0:1024]
    ps1b = psAb[:, 1024:2048]
    ps1 = psA.ap()[:, 512:1024]

    sems = {n: Sem(nc, n) for n in ("sq_x", "sq_s", "sq_w", "sc_v", "sc_s", "sc_t", "sc_g")}
    EV = {}

    def walk(en, holder):
        class _P:
            def __getattr__(self, a):
                return getattr(holder[0], a)
        eng = _P()
        yield
        sq_x, sq_s, sq_w = sems["sq_x"], sems["sq_s"], sems["sq_w"]
        sc_v, sc_s, sc_t, sc_g = sems["sc_v"], sems["sc_s"], sems["sc_t"], sems["sc_g"]

        def ev(key):
            v = EV.get(key)
            if v is None:
                v = EVp.get(key)
            if final:
                assert v is not None, key
            return v

        def W(key):
            v = ev(key) if isinstance(key, str) else key
            if v is not None and v[1] > 0:
                eng.wait_ge(sems[v[0]].h, v[1])

        def I(sem, instr, key=None, k=1):
            instr.then_inc(sem.h, k)
            sem.n += k
            if key is not None:
                EV[key] = (sem.name, sem.n)
            return (sem.name, sem.n)

        def rt(sem, instr, key=None):
            v = I(sem, instr, key)
            eng.wait_ge(sem.h, v[1])
            return v

        def dma(q, sem, out, in_, key=None, **kw):
            q.dma_start(out, in_, **kw).then_inc(sem.h, 16)
            sem.n += 16
            if key is not None:
                EV[key] = (sem.name, sem.n)
            return (sem.name, sem.n)

        led = {"ps0b": None, "ps1b": None}

        if en == "gpsimd":
            g = eng
            rt(sc_g, g.memset(ident[:, :], 1.0))
            g.affine_select(ident[:, :], ident[:, :], [[-1, 128]], ALU.is_equal, 0.0, base=0, channel_multiplier=1)
            rt(sc_g, g.memset(identf[:, :], 1.0))
            g.affine_select(identf[:, :], identf[:, :], [[-1, 128]], ALU.is_equal, 0.0, base=0, channel_multiplier=1)
            rt(sc_g, g.memset(masks[:, :], 1.0))
            for t in range(4):
                rt(sc_g, g.affine_select(masks[:, 512 * t:512 * (t + 1)], masks[:, 512 * t:512 * (t + 1)],
                                         [[1, 512]], ALU.is_ge, 0.0, base=-128 * t, channel_multiplier=-1))
            g.memset(vsb[:, :].rearrange("p (h c) -> p h c", h=H)[:, :, 64:65], 1.0)
            rt(sc_g, g.iota(posi[:, :], [[128, NT]], base=0, channel_multiplier=1))
            I(sc_g, g.tensor_copy(posf[:, :], posi[:, :]), "p0g")
            dma(g, sq_w, wrsb[:, :].rearrange("p (c e) -> p c e", c=NC_),
                wr_h.rearrange("(c p) e -> p c e", p=128), "wr")
        if en == "sync":
            dma(eng, sems["sq_x"], invfb[:, :], invf_h, "invfb")
        if en == "vector":
            I(sc_v, eng.memset(ones_f[:, :], 1.0), "ones")

        for t in range(NT):
            if en == "vector":
                v = eng
                if t == 0:
                    W("invfb"); W("p0g")
                else:
                    W(f"tab{t-1}")
                rt(sc_v, v.tensor_scalar(angb[:, :], invfb[:, :], posf[:, t:t + 1], None, ALU.mult))
                rt(sc_v, v.tensor_scalar(ubb[:, :], angb[:, :], INV2PI, 8388608.0, ALU.mult, ALU.add))
                rt(sc_v, v.tensor_scalar(nbb[:, :], ubb[:, :], -8388608.0, None, ALU.add))
                rt(sc_v, v.scalar_tensor_tensor(tb2[:, :], nbb[:, :], -HI2PI, angb[:, :], ALU.mult, ALU.add))
                rt(sc_v, v.scalar_tensor_tensor(tb2[:, :], nbb[:, :], -LO2PI, tb2[:, :], ALU.mult, ALU.add))
                rt(sc_v, v.tensor_scalar(ubb[:, :], angb[:, :], INV2PI, 0.25, ALU.mult, ALU.add))
                rt(sc_v, v.tensor_scalar(ubb[:, :], ubb[:, :], 8388608.0, None, ALU.add))
                rt(sc_v, v.tensor_scalar(nbb[:, :], ubb[:, :], -8388608.0, None, ALU.add))
                rt(sc_v, v.scalar_tensor_tensor(ubb[:, :], nbb[:, :], -HI2PI, angb[:, :], ALU.mult, ALU.add))
                rt(sc_v, v.scalar_tensor_tensor(ubb[:, :], nbb[:, :], -LO2PI, ubb[:, :], ALU.mult, ALU.add))
                rt(sc_v, v.tensor_scalar(ubb[:, :], ubb[:, :], PI / 2, None, ALU.add), f"ang{t}")
            if en == "scalar":
                s = eng
                W(f"ang{t}")
                s.activation(tabs[:, 1024 + 32 * t:1056 + 32 * t], tb2[:, :], AF.Sin)
                I(sc_s, s.activation(tabs[:, 32 * t:32 * t + 32], ubb[:, :], AF.Sin), f"tab{t}")
        if en == "vector":
            v = eng
            W(f"tab{NT-1}")
            v.tensor_scalar(tabs[:, 2048:3072], tabs[:, 0:1024], 0.125, None, ALU.mult)
            I(sc_v, v.tensor_scalar(tabs[:, 3072:4096], tabs[:, 1024:2048], 0.125, None, ALU.mult), "tabs")

        for i, (wh, wT) in enumerate(((wq_h, wqT), (wk_h, wkT), (wv_h, wvT))):
            if en == "gpsimd":
                if i > 0:
                    W(f"wtre{i-1}_63")
                dma(eng, sems["sq_w"], stg_w[:, :].rearrange("p (c d) -> p c d", c=8),
                    wh.rearrange("(c p) d -> p c d", p=128), f"wst{i}")
            for k in range(64):
                ec, dc = k // 8, k % 8
                bank = "ps0b" if k % 2 == 0 else "ps1b"
                pd = ps0b if k % 2 == 0 else ps1b
                if en == "tensor":
                    if k == 0:
                        W(f"wst{i}")
                        W("p0g")
                    W(led[bank])
                    I(sc_t, eng.matmul(pd[:, 0:128], stg_w[:, ec * 1024 + dc * 128:ec * 1024 + dc * 128 + 128],
                                       ident[:, :], is_transpose=True), f"wtrm{i}_{k}")
                if en == "scalar":
                    W(f"wtrm{i}_{k}")
                    I(sc_s, eng.copy(wT[:, dc * 1024 + ec * 128:dc * 1024 + ec * 128 + 128], pd[:, 0:128]),
                      f"wtre{i}_{k}")
                led[bank] = f"wtre{i}_{k}"

        if en == "sync":
            W("wtre2_63")
            dma(eng, sems["sq_x"], xb[:, 0:1024], x_h[0:128, :], "x0")
        if "p0" in FENCES:
            yield

        for t in range(NT):
            if "tile" in FENCES:
                yield
            xv = xb[:, (t % 2) * 1024:(t % 2) * 1024 + 1024]
            if en == "sync":
                if t + 1 < NT:
                    if t >= 1:
                        W(f"xn{t-1}"); W(f"sq{t-1}")
                    dma(eng, sems["sq_x"], xb[:, ((t + 1) % 2) * 1024:((t + 1) % 2) * 1024 + 1024],
                        x_h[128 * (t + 1):128 * (t + 2), :], f"x{t+1}")
            if en == "scalar":
                s = eng
                W(f"x{t}")
                if t >= 1:
                    W(f"sqfree{t-1}")
                I(sc_s, s.activation(sqb[:, :], xv, AF.Square), f"sq{t}")
            if en == "vector":
                v = eng
                W(f"x{t}")
                v.reduce_sum(smalls[:, 0:1], xv, axis=AX.X)
                W(f"sq{t}")
                rt(sc_v, v.reduce_sum(smalls[:, 1:2], sqb[:, :], axis=AX.X), f"sqfree{t}")
                rt(sc_v, v.tensor_scalar(smalls[:, 0:1], smalls[:, 0:1], 1.0 / D, None, ALU.mult))
                rt(sc_v, v.tensor_scalar(smalls[:, 1:2], smalls[:, 1:2], 1.0 / D, None, ALU.mult))
                rt(sc_v, v.tensor_tensor(smalls[:, 2:3], smalls[:, 0:1], smalls[:, 0:1], ALU.mult))
                rt(sc_v, v.tensor_tensor(smalls[:, 1:2], smalls[:, 1:2], smalls[:, 2:3], ALU.subtract))
                I(sc_v, v.tensor_scalar(smalls[:, 1:2], smalls[:, 1:2], 1e-5, None, ALU.add), f"var{t}")
            if en == "scalar":
                s = eng
                W(f"var{t}")
                rt(sc_s, s.activation(smalls[:, 3:4], smalls[:, 1:2], AF.Ln))
                I(sc_s, s.activation(smalls[:, 3:4], smalls[:, 3:4], AF.Exp, scale=-0.5), f"rstd{t}")
            if en == "vector":
                v = eng
                W(f"rstd{t}")
                if t >= 1:
                    W(f"xnTmm{t-1}")
                I(sc_v, v.tensor_scalar(xnb[:, :], xv, smalls[:, 0:1], smalls[:, 3:4],
                                        ALU.subtract, ALU.mult), f"xn{t}")
            for c in range(NC_):
                bank = "ps0b" if c % 2 == 0 else "ps1b"
                pd = ps0b if c % 2 == 0 else ps1b
                if en == "tensor":
                    te = eng
                    if c == 0:
                        W(f"xn{t}")
                        if t == 0:
                            W("tabs")
                    W(led[bank])
                    key = f"xnTm{t}_{c}"
                    I(sc_t, te.matmul(pd[:, 0:128], xnb[:, c * 128:c * 128 + 128], ident[:, :],
                                      is_transpose=True), key)
                    if c == NC_ - 1:
                        EV[f"xnTmm{t}"] = EV[key]
                if en == "scalar":
                    W(f"xnTm{t}_{c}")
                    I(sc_s, eng.copy(xnT[:, c * S + t * 128:c * S + t * 128 + 128], pd[:, 0:128]), f"xnTe{t}_{c}")
                led[bank] = f"xnTe{t}_{c}"

            if en == "tensor":
                te = eng
                for mi, (psX, wT) in enumerate(((ps2, wqT), (ps3, wkT), (ps4, wvT))):
                    W(f"xnTe{t}_{NC_-1}")
                    if mi == 0 and t == 0:
                        W("wtre2_63")
                    if t >= 1:
                        W([f"rope_q{t-1}", f"rope_k{t-1}", f"wrout{t-1}"][mi])
                    for c in range(NC_):
                        for hf in range(2):
                            last = te.matmul(psX[:, hf * 512:hf * 512 + 512],
                                             xnT[:, c * S + t * 128:c * S + t * 128 + 128],
                                             wT[:, c * 1024 + hf * 512:c * 1024 + hf * 512 + 512],
                                             start=(c == 0), stop=(c == NC_ - 1))
                    I(sc_t, last, f"mm{'qkv'[mi]}{t}")
            if en == "vector":
                v = eng
                for name, ps, rob, c0 in (("q", ps2, qro, 2048), ("k", ps3, kro, 0)):
                    W(f"mm{name}{t}")
                    if t >= 1:
                        W(f"{name}Tmm{t-1}")
                    r_in = ps.ap().rearrange("p (h f two) -> p h f two", h=H, two=2)
                    te_, to_ = r_in[:, :, :, 0:1], r_in[:, :, :, 1:2]
                    r_out = rob[:, :].rearrange("p (h f two) -> p h f two", h=H, two=2)
                    re_, ro_ = r_out[:, :, :, 0:1], r_out[:, :, :, 1:2]
                    cosb = tabs[:, c0 + 32 * t:c0 + 32 * t + 32].unsqueeze(1).broadcast_to([128, H, 32]).unsqueeze(3)
                    sinb = tabs[:, c0 + 1024 + 32 * t:c0 + 1056 + 32 * t].unsqueeze(1).broadcast_to([128, H, 32]).unsqueeze(3)
                    tA = tmA[:, :].rearrange("p (h f) -> p h f", h=H).unsqueeze(3)
                    tB = tmB[:, :].rearrange("p (h f) -> p h f", h=H).unsqueeze(3)
                    v.tensor_tensor(tA, te_, cosb, ALU.mult)
                    rt(sc_v, v.tensor_tensor(tB, to_, sinb, ALU.mult))
                    rt(sc_v, v.tensor_tensor(re_, tA, tB, ALU.subtract))
                    v.tensor_tensor(tA, te_, sinb, ALU.mult)
                    rt(sc_v, v.tensor_tensor(tB, to_, cosb, ALU.mult))
                    rt(sc_v, v.tensor_tensor(ro_, tA, tB, ALU.add), f"rope_{name}{t}")
            if en == "scalar":
                s = eng
                W(f"mmv{t}")
                if t >= 1:
                    W(f"vst{t-1}")
                for h in range(H):
                    last = s.copy(vsb[:, h * 65:h * 65 + 64], ps4[:, h * 64:h * 64 + 64])
                v_ev = I(sc_s, last, f"vev{t}")
                eng.wait_ge(sc_s.h, v_ev[1])
                dma(s, sems["sq_s"], v_s[t], vsb[:, :], f"vst{t}")
            if en == "tensor":
                te = eng
                W(f"vev{t}")
                if t == 0:
                    W("wr")
                for c in range(NC_):
                    last = te.matmul(ps4[:, 0:8], xnT[:, c * S + t * 128:c * S + t * 128 + 128],
                                     wrsb[:, c * E:c * E + E], start=(c == 0), stop=(c == NC_ - 1))
                I(sc_t, last, f"mmr{t}")
            if en == "vector":
                W(f"mmr{t}")
                I(sc_v, eng.reduce_max(smalls[:, 4:5], ps4[:, 0:8], axis=AX.X, negate=True), f"rmax{t}")
            if en == "scalar":
                W(f"rmax{t}")
                I(sc_s, eng.activation(rsm[:, :], ps4[:, 0:8], AF.Exp, bias=smalls[:, 4:5]), f"rexp{t}")
            if en == "vector":
                v = eng
                W(f"rexp{t}")
                rt(sc_v, v.reduce_sum(smalls[:, 5:6], rsm[:, :], axis=AX.X))
                rt(sc_v, v.reciprocal(smalls[:, 6:7], smalls[:, 5:6]))
                rt(sc_v, v.tensor_scalar(rsm[:, :], rsm[:, :], smalls[:, 6:7], None, ALU.mult))
                rt(sc_v, v.reduce_max(smalls[:, 7:8], rsm[:, :], axis=AX.X))
                rt(sc_v, v.tensor_scalar(rsm2[:, :], rsm[:, :], smalls[:, 7:8], None, ALU.is_equal))
                rt(sc_v, v.scalar_tensor_tensor(rsm3[:, :], rsm2[:, :], -1e30, rsm[:, :], ALU.mult, ALU.add))
                I(sc_v, v.reduce_max(smalls[:, 8:9], rsm3[:, :], axis=AX.X), f"rtop{t}")
            if en == "scalar":
                s = eng
                W(f"rtop{t}")
                s.activation(smalls[:, 9:10], smalls[:, 7:8], AF.Exp)
                I(sc_s, s.activation(smalls[:, 10:11], smalls[:, 8:9], AF.Exp), f"rexp2{t}")
            if en == "vector":
                v = eng
                W(f"rexp2{t}")
                rt(sc_v, v.tensor_tensor(smalls[:, 11:12], smalls[:, 9:10], smalls[:, 10:11], ALU.add))
                rt(sc_v, v.reciprocal(smalls[:, 12:13], smalls[:, 11:12]))
                rt(sc_v, v.tensor_scalar(rsm2[:, :], rsm2[:, :], smalls[:, 9:10], smalls[:, 12:13], ALU.mult, ALU.mult))
                rt(sc_v, v.tensor_scalar(rsm4[:, :], rsm3[:, :], smalls[:, 8:9], None, ALU.is_equal))
                rt(sc_v, v.tensor_scalar(rsm4[:, :], rsm4[:, :], smalls[:, 10:11], smalls[:, 12:13], ALU.mult, ALU.mult))
                I(sc_v, v.tensor_tensor(w_rout[:, t * E:t * E + E], rsm2[:, :], rsm4[:, :], ALU.add), f"wrout{t}")
            if en == "scalar":
                W(f"rope_q{t}")
                I(sc_s, eng.activation(sqb[:, :], qro[:, :], AF.Square), f"sqq{t}")
            if en == "vector":
                W(f"sqq{t}")
                I(sc_v, eng.reduce_sum(qn2t[:, :], sqb[:, :].rearrange("p (h f) -> p h f", h=H), axis=AX.X), f"qn2{t}")
            if en == "tensor":
                W(f"qn2{t}")
                W(led["ps1b"])
                I(sc_t, eng.transpose(ps1[0:16, 0:128], qn2t[:, :], identf[:, :]), f"qn2T{t}")
            if en == "vector":
                W(f"qn2T{t}")
                I(sc_v, eng.reduce_max(Mbuf[:, t:t + 1], ps1[0:16, 0:128], axis=AX.X), f"Mb{t}")
            led["ps1b"] = f"Mb{t}"
            if en == "scalar":
                W(f"rope_k{t}")
                W(f"qn2{t}")
                I(sc_s, eng.activation(sqb[:, :], kro[:, :], AF.Square), f"sqk{t}")
            if en == "vector":
                v = eng
                W(f"sqk{t}")
                rt(sc_v, v.reduce_sum(kn2t[:, :], sqb[:, :].rearrange("p (h f) -> p h f", h=H), axis=AX.X))
                I(sc_v, v.reduce_max(kmaxb[:, t:t + 1], kn2t[:, :], axis=AX.X), f"kn2{t}")
            for name, rob, tile, store in (("q", qro, qTt, qT_s), ("k", kro, kTt, kT_s)):
                for c in range(NC_):
                    bank = "ps0b" if c % 2 == 0 else "ps1b"
                    pd = ps0b if c % 2 == 0 else ps1b
                    if en == "tensor":
                        W(led[bank])
                        key = f"{name}Tm{t}_{c}"
                        I(sc_t, eng.matmul(pd[:, 0:128], rob[:, c * 128:c * 128 + 128], ident[:, :],
                                           is_transpose=True), key)
                        if c == NC_ - 1:
                            EV[f"{name}Tmm{t}"] = EV[key]
                    if en == "scalar":
                        W(f"{name}Tm{t}_{c}")
                        if c == 0 and t >= 1:
                            W(f"{name}st{t-1}")
                        I(sc_s, eng.copy(tile[:, c * 128:c * 128 + 128], pd[:, 0:128]), f"{name}Te{t}_{c}")
                    led[bank] = f"{name}Te{t}_{c}"
                if en == "scalar":
                    W(f"{name}Te{t}_{NC_-1}")
                    dma(eng, sems["sq_s"], store[t], tile[:, :], f"{name}st{t}")

        if en == "sync":
            for i_ in range(4):
                dma(eng, sems["sq_x"], out_h[i_ * 1024:(i_ + 1) * 1024, :],
                    x_h[i_ * 1024:(i_ + 1) * 1024, :], "xcopy" if i_ == 3 else None)

        if "p15" in FENCES:
            yield
        if en == "tensor":
            W(f"kn2{NT-1}")
            W(led["ps1b"])
            I(sc_t, eng.transpose(ps1[0:32, 0:128], kmaxb[:, :], identf[:, :]), "km1")
        if en == "vector":
            W("km1")
            I(sc_v, eng.reduce_max(kk1[:, :], ps1[0:32, 0:128], axis=AX.X), "km2")
        if en == "tensor":
            W("km2")
            I(sc_t, eng.transpose(ps1[0:1, 0:32], kk1[:, :], identf[0:32, 0:32]), "km3")
        if en == "vector":
            v = eng
            W("km3")
            I(sc_v, v.reduce_max(kk2[:, :], ps1[0:1, 0:32], axis=AX.X), "km4")
            W(f"Mb{NT-1}")
            I(sc_v, v.reduce_max(qn2blk[:, :], Mbuf[:, :].rearrange("p (b f) -> p b f", b=8), axis=AX.X), "qbm")
        if en == "tensor":
            W("km4")
            I(sc_t, eng.matmul(ps1[0:16, 0:1], ones_f[0:1, 0:16], kk2[0:1, :], start=True, stop=True), "km5")
        if en == "vector":
            v = eng
            W("km5")
            rt(sc_v, v.tensor_copy(maxk_col[:, :], ps1[0:16, 0:1]), "km6")
            W("qbm")
            I(sc_v, v.tensor_scalar(qn2blk[:, :], qn2blk[:, :], maxk_col[:, 0:1], None, ALU.mult), "prod")
        led["ps1b"] = "km6"
        if en == "scalar":
            s = eng
            W("prod")
            rt(sc_s, s.activation(negm16[:, :], qn2blk[:, :], AF.Ln))
            I(sc_s, s.activation(negm16[:, :], negm16[:, :], AF.Exp, scale=0.5), "negm_a")
        if en == "vector":
            W("negm_a")
            I(sc_v, eng.tensor_scalar(negm16[:, :], negm16[:, :], -1.01, None, ALU.mult), "negm")
        if en == "sync":
            sync = eng
            W("negm")
            v1 = dma(sync, sems["sq_x"], negm_d.rearrange("(h e) -> h e", h=16), negm16[:, :])
            sync.wait_ge(sems["sq_x"].h, v1[1])
            dma(sync, sems["sq_x"], negm_fl[0:1, :], negm_d.rearrange("(o f) -> o f", o=1), "negm_fl")
        if en == "tensor":
            W("negm_fl")
            W(led["ps1b"])
            I(sc_t, eng.matmul(ps1[:, 0:128], ones_f[0:1, :], negm_fl[0:1, :], start=True, stop=True), "biasT_mm")
        if en == "vector":
            W("biasT_mm")
            I(sc_v, eng.tensor_copy(biasT[:, :], ps1[:, 0:128]), "biasT")
        led["ps1b"] = "biasT"

        g_acc_ev = [None] * NT
        last_sil = None
        last_hm = None
        last_dn = None
        last_stg = {0: None, 1: None}
        last_gacc_stg = {0: None, 1: None}
        for e in range(E):
            if "exp" in FENCES:
                yield
            if en == "gpsimd":
                g = eng
                if e == 0:
                    W("biasT")
                    W(f"kst{NT-1}")
                else:
                    W(f"gu_done{e-1}")
                dma(g, sems["sq_w"], wgsb[:, :].rearrange("p (c f) -> p c f", c=NC_),
                    wg_h[e].rearrange("(c p) f -> p c f", p=128))
                dma(g, sems["sq_w"], wusb[:, :].rearrange("p (c f) -> p c f", c=NC_),
                    wu_h[e].rearrange("(c p) f -> p c f", p=128), f"wgu{e}")
                if e >= 1:
                    W(f"dn_done{e-1}")
                dma(g, sems["sq_w"], wdsb[:, :].rearrange("p (f d) -> p f d", f=NFC),
                    wd_h[e].rearrange("(f p) d -> p f d", p=128), f"wd{e}")
            for grp in range(NGRP):
                t0 = grp * GRP
                for fc in range(NFC):
                    if en == "tensor":
                        te = eng
                        if fc == 0 and grp == 0:
                            W(f"wgu{e}")
                            if e == 0:
                                W(f"xnTe{NT-1}_{NC_-1}")
                                W(f"wrout{NT-1}")
                        W(last_sil)
                        for c in range(NC_):
                            te.matmul(ps4[:, 0:512], wgsb[:, c * FF + fc * 128:c * FF + fc * 128 + 128],
                                      xnT[:, c * S + t0:c * S + t0 + GRP], start=(c == 0), stop=(c == NC_ - 1))
                        W(last_hm)
                        for c in range(NC_):
                            last = te.matmul(ps4[:, 512:1024], wusb[:, c * FF + fc * 128:c * FF + fc * 128 + 128],
                                             xnT[:, c * S + t0:c * S + t0 + GRP], start=(c == 0), stop=(c == NC_ - 1))
                        key = f"gu{e}_{grp}_{fc}"
                        I(sc_t, last, key)
                        if grp == NGRP - 1 and fc == NFC - 1:
                            EV[f"gu_done{e}"] = EV[key]
                    if en == "scalar":
                        W(f"gu{e}_{grp}_{fc}")
                        I(sc_s, eng.activation(hsil[:, :], ps4[:, 0:512], AF.Silu), f"sil{e}_{grp}_{fc}")
                    last_sil = f"sil{e}_{grp}_{fc}"
                    if en == "vector":
                        W(f"sil{e}_{grp}_{fc}")
                        W(last_dn)
                        I(sc_v, eng.tensor_tensor(hT[:, fc * GRP:fc * GRP + GRP], ps4[:, 512:1024], hsil[:, :],
                                                  ALU.mult), f"hm{e}_{grp}_{fc}")
                    last_hm = f"hm{e}_{grp}_{fc}"
                for ti in range(4):
                    tt = grp * 4 + ti
                    par = ti % 2
                    psD = ps3 if par == 0 else ps2
                    if en == "tensor":
                        te = eng
                        if ti == 0:
                            W(f"hm{e}_{grp}_{NFC-1}")
                            if grp == 0:
                                W(f"wd{e}")
                                if e == 0:
                                    W(f"rope_q{NT-1}")
                                    W(f"rope_k{NT-1}")
                        W(last_stg[par])
                        for fc in range(NFC):
                            for hf in range(2):
                                last = te.matmul(psD[:, hf * 512:hf * 512 + 512],
                                                 hT[:, fc * GRP + ti * 128:fc * GRP + ti * 128 + 128],
                                                 wdsb[:, fc * 1024 + hf * 512:fc * 1024 + hf * 512 + 512],
                                                 start=(fc == 0), stop=(fc == NFC - 1))
                        key = f"dnm{e}_{tt}"
                        I(sc_t, last, key)
                        if ti == 3:
                            EV[f"dn{e}_{grp}"] = EV[key]
                            if grp == NGRP - 1:
                                EV[f"dn_done{e}"] = EV[key]
                    if ti == 3:
                        last_dn = f"dn{e}_{grp}"
                    if en == "vector":
                        W(f"dnm{e}_{tt}")
                        W(last_gacc_stg[tt % 2])
                        I(sc_v, eng.tensor_scalar(stg[:, (tt % 2) * 1024:(tt % 2) * 1024 + 1024], psD[:, :],
                                                  w_rout[:, tt * E + e:tt * E + e + 1], None, ALU.mult),
                          f"stg{e}_{tt}")
                    last_stg[par] = f"stg{e}_{tt}"
                    if en == "gpsimd":
                        g = eng
                        W(f"stg{e}_{tt}")
                        W(g_acc_ev[tt])
                        kw = dict(accum_op=ALU.add) if e >= 1 else {}
                        dma(g, sems["sq_w"], g_s[tt * 128:tt * 128 + 128, :],
                            stg[:, (tt % 2) * 1024:(tt % 2) * 1024 + 1024], f"gacc{e}_{tt}", **kw)
                    g_acc_ev[tt] = f"gacc{e}_{tt}"
                    last_gacc_stg[tt % 2] = f"gacc{e}_{tt}"

        P2_T = f"dn_done{E-1}"
        P2_V = f"stg{E-1}_{NT-1}"
        P2_G = f"gacc{E-1}_{NT-1}"

        if en == "gpsimd":
            g = eng
            W(P2_T); W(P2_V)
            dma(g, sems["sq_w"], stg_wo[:, :].rearrange("p (c d) -> p c d", c=8),
                wo_h.rearrange("(c p) d -> p c d", p=128), "wo_st")
        if en == "sync":
            sync = eng
            W(P2_T)
            for t in range(NT):
                dma(sync, sems["sq_x"], kTb[:, t * 1024:t * 1024 + 1024], kT_s[t],
                    "kT_ld" if t == NT - 1 else None)
        if en == "scalar":
            s = eng
            W(P2_T); W(P2_V); W(P2_G)
            for t in range(NT):
                dma(s, sems["sq_s"], vsb3[:, t * H * 65:(t + 1) * H * 65], v_s[t],
                    "v_ld" if t == NT - 1 else None)
        for k in range(64):
            ec, dc = k // 8, k % 8
            bank = "ps0b" if k % 2 == 0 else "ps1b"
            pd = ps0b if k % 2 == 0 else ps1b
            if en == "tensor":
                if k == 0:
                    W("wo_st")
                W(led[bank])
                I(sc_t, eng.matmul(pd[:, 0:128], stg_wo[:, ec * 1024 + dc * 128:ec * 1024 + dc * 128 + 128],
                                   ident[:, :], is_transpose=True), f"woTm{k}")
            if en == "scalar":
                W(f"woTm{k}")
                I(sc_s, eng.copy(woT[:, dc * 1024 + ec * 128:dc * 1024 + ec * 128 + 128], pd[:, 0:128]),
                  f"woTe{k}")
            led[bank] = f"woTe{k}"

        def emit_wo_tile(qbw, ti):
            tt = 4 * qbw + ti
            if en == "sync":
                for pf in ([tt, tt + 1] if ti == 0 else ([tt + 1] if ti < 3 else [])):
                    if 4 * qbw <= pf < 4 * qbw + 4:
                        if pf >= 2:
                            W(f"ost{pf-2}")
                        dma(eng, sems["sq_x"], gb3[:, (pf % 2) * 1024:(pf % 2) * 1024 + 1024],
                            g_s[pf * 128:pf * 128 + 128, :], f"g_ld{pf}")
            if en == "scalar":
                W(f"g_ld{tt}")
                if tt >= 1:
                    W(f"comb{tt-1}")
                I(sc_s, eng.activation(gb3s[:, :], gb3[:, (tt % 2) * 1024:(tt % 2) * 1024 + 1024],
                                       AF.Silu), f"gsil{tt}")
            if en == "tensor":
                te = eng
                W(f"ont{qbw}_{H-1}")
                if tt == 0:
                    W("woTe63")
                if tt >= 1:
                    W(f"comb{tt-1}")
                for dc in range(NC_):
                    for hf in range(2):
                        last = te.matmul(ps4[:, hf * 512:hf * 512 + 512],
                                         ontT[:, (qbw % 2) * 4096 + dc * 512 + ti * 128:(qbw % 2) * 4096 + dc * 512 + ti * 128 + 128],
                                         woT[:, dc * 1024 + hf * 512:dc * 1024 + hf * 512 + 512],
                                         start=(dc == 0), stop=(dc == NC_ - 1))
                I(sc_t, last, f"womm{tt}")
            if en == "vector":
                v = eng
                W(f"womm{tt}")
                W(f"gsil{tt}")
                if tt >= 1:
                    W(f"ost{tt-1}")
                I(sc_v, v.tensor_tensor(gb3[:, (tt % 2) * 1024:(tt % 2) * 1024 + 1024],
                                        ps4[:, :], gb3s[:, :], ALU.mult), f"comb{tt}")
            if en == "gpsimd":
                g = eng
                W(f"comb{tt}")
                if tt == 0:
                    W("xcopy")
                dma(g, sems["sq_w"], out_h[tt * 128:tt * 128 + 128, :],
                    gb3[:, (tt % 2) * 1024:(tt % 2) * 1024 + 1024], f"ost{tt}", accum_op=ALU.add)

        pair_i = 0
        for qb in range(NQB):
            if "qb" in FENCES:
                yield
            if en == "sync":
                sync = eng
                if qb == 0:
                    W(P2_T)
                else:
                    W(f"sc_last{qb-1}")
                dma(sync, sems["sq_x"], qblk[:, :].rearrange("p (t c) -> p t c", t=4),
                    qT_s[4 * qb:4 * qb + 4].transpose([1, 0, 2]), f"q_ld{qb}")
            npair = 2 * qb + 2
            for h in range(H):
                if qb >= 1 and 1 <= h <= 4:
                    emit_wo_tile(qb - 1, h - 1)
                hp, hr = h // 2, h % 2
                for pp in range(npair):
                    cp = pair_i
                    pair_i += 1
                    c0, c1_ = 2 * pp, 2 * pp + 1
                    boundary = c0 >= 4 * qb
                    psS = psA.ap() if cp % 2 == 0 else ps2.ap()
                    pth = Pt[:, (cp % 2) * 1024:(cp % 2) * 1024 + 1024]
                    if en == "tensor":
                        te = eng
                        if h == 0 and pp == 0:
                            W("kT_ld")
                            W(f"q_ld{qb}")
                            if qb == 0:
                                W("biasT")
                                W(P2_V)
                        if cp >= 2:
                            W(f"expP{cp-2}")
                        rq = qblk[hr * 64:hr * 64 + 64, :].rearrange("p (t c2) -> p t c2", t=4)[:, :, hp * 128:hp * 128 + 128]
                        te.matmul(psS[:, 0:512], kTb[hr * 64:hr * 64 + 64, c0 * 1024 + hp * 128:c0 * 1024 + hp * 128 + 128],
                                  rq, start=True, stop=True)
                        I(sc_t, te.matmul(psS[:, 512:1024], kTb[hr * 64:hr * 64 + 64, c1_ * 1024 + hp * 128:c1_ * 1024 + hp * 128 + 128],
                                          rq, start=True, stop=True), f"smmP{cp}")
                    if en == "scalar":
                        W(f"smmP{cp}")
                        if cp >= 2:
                            W(f"avP{cp-2}")
                        I(sc_s, eng.activation(pth, psS[:, :], AF.Exp,
                                               bias=biasT[:, h * 8 + qb:h * 8 + qb + 1]), f"expP{cp}")
                    if boundary and en == "vector":
                        W(f"expP{cp}")
                        tloc = c0 - 4 * qb
                        I(sc_v, eng.tensor_tensor(pth, pth, masks[:, tloc * 512:tloc * 512 + 1024], ALU.mult),
                          f"mskP{cp}")
                    if en == "tensor":
                        te = eng
                        W(f"mskP{cp}" if boundary else f"expP{cp}")
                        if pp == 0:
                            if h >= 2:
                                W(f"ont{qb}_{h-2}")
                            elif qb >= 1:
                                W(f"ont{qb-1}_{H-2+h}")
                            if h == 0 and qb == 0:
                                W("v_ld")
                        psV = ps3.ap()[0:65, (h % 2) * 512:(h % 2) * 512 + 512]
                        te.matmul(psV, vsb3[:, c0 * H * 65 + h * 65:c0 * H * 65 + h * 65 + 65],
                                  pth[:, 0:512], start=(c0 == 0), stop=False)
                        key = f"avP{cp}"
                        I(sc_t, te.matmul(psV, vsb3[:, c1_ * H * 65 + h * 65:c1_ * H * 65 + h * 65 + 65],
                                          pth[:, 512:1024], start=False, stop=(c1_ == 4 * qb + 3)), key)
                        if c1_ == 4 * qb + 3:
                            EV[f"av{qb}_{h}"] = EV[key]
                            if h == H - 1:
                                EV[f"sc_last{qb}"] = EV[key]
                if en == "scalar":
                    s = eng
                    W(f"av{qb}_{h}")
                    psF = ps3.ap()
                    rt(sc_s, s.activation(lnrow[0:1, :], psF[64:65, (h % 2) * 512:(h % 2) * 512 + 512], AF.Ln))
                    I(sc_s, s.activation(rdrow[0:1, :], lnrow[0:1, :], AF.Exp, scale=-1.0), f"rd{qb}_{h}")
                if en == "tensor":
                    W(f"rd{qb}_{h}")
                    if qb == 0 and h < 2:
                        W(led["ps1b" if h % 2 else "ps0b"])
                    I(sc_t, eng.matmul(ps3.ap()[64:128, (h % 2) * 512:(h % 2) * 512 + 512],
                                       ones_f[0:1, 0:64], rdrow[0:1, :], start=True, stop=True),
                      f"rdbcm{qb}_{h}")
                if en == "vector":
                    v = eng
                    W(f"rdbcm{qb}_{h}")
                    if h == 0 and qb >= 2:
                        W(f"womm{4*(qb-1)-1}")
                    rt(sc_v, v.tensor_copy(rdbc[:, :], ps3.ap()[64:128, (h % 2) * 512:(h % 2) * 512 + 512]),
                       f"rdbce{qb}_{h}")
                    I(sc_v, v.tensor_tensor(ontT[hr * 64:hr * 64 + 64, (qb % 2) * 4096 + hp * 512:(qb % 2) * 4096 + hp * 512 + 512],
                                            ps3[0:64, (h % 2) * 512:(h % 2) * 512 + 512],
                                            rdbc[0:64, :], ALU.mult), f"ont{qb}_{h}")
            if qb == NQB - 1:
                for ti in range(4):
                    emit_wo_tile(qb, ti)

        if en == "scalar":
            EV["tot_sq_s"] = ("sq_s", sems["sq_s"].n)
        if en == "gpsimd":
            EV["tot_sq_w"] = ("sq_w", sems["sq_w"].n)
        if en == "sync":
            eng.wait_ge(sems["sq_x"].h, sems["sq_x"].n)
            W("tot_sq_s")
            W("tot_sq_w")

    n_fence = 1 + (1 if "p0" in FENCES else 0) + (NT if "tile" in FENCES else 0) \
        + (1 if "p15" in FENCES else 0) + (E if "exp" in FENCES else 0) + (NQB if "qb" in FENCES else 0)
    ENG = ("sync", "gpsimd", "scalar", "vector", "tensor")
    holders = {en: [None] for en in ENG}
    gens = {en: walk(en, holders[en]) for en in ENG}
    for en in ENG:
        next(gens[en])
    with nc.Block() as blk:
        for seg in range(n_fence):
            for en in ENG:
                def run(engobj, en=en):
                    holders[en][0] = engobj
                    try:
                        next(gens[en])
                    except StopIteration:
                        pass
                getattr(blk, en)(run)
    return nc, EV


def build(dbg=()):
    global EVp
    EVp = {}
    _, ev1 = build_once(dbg, EVp, final=False)
    EVp = ev1
    nc, ev2 = build_once(dbg, EVp, final=True)
    assert ev1 == ev2, "event mismatch between passes"
    return nc


EVp = {}
_NC_CACHE = {}


def kernel(**inputs):
    if "nc" not in _NC_CACHE:
        _NC_CACHE["nc"] = build()
    nc = _NC_CACHE["nc"]
    x = np.ascontiguousarray(inputs["L0_attn_x"])
    base = {
        "wq": np.ascontiguousarray(inputs["L0_attn_Wq"]),
        "wk": np.ascontiguousarray(inputs["L0_attn_Wk"]),
        "wv": np.ascontiguousarray(inputs["L0_attn_Wv"]),
        "wo": np.ascontiguousarray(inputs["L0_attn_Wo"]),
        "wr": np.ascontiguousarray(inputs["L0_ffn_kernel_router_DE"]),
        "wg": np.ascontiguousarray(inputs["L0_ffn_kernel_gating_EDF"]),
        "wu": np.ascontiguousarray(inputs["L0_ffn_kernel_up_EDF"]),
        "wd": np.ascontiguousarray(inputs["L0_ffn_kernel_down_EFD"]),
        "invf": np.ascontiguousarray(np.broadcast_to(
            (10000.0 ** (-np.arange(0, 64, 2, dtype=np.float64) / 64.0)).astype(np.float32)[None, :],
            (128, 32))),
    }
    in_maps = [dict(base, x=np.ascontiguousarray(x[b])) for b in range(8)]
    res = run_bass_kernel_spmd(nc, in_maps, core_ids=list(range(8)))
    return np.stack([res.results[b]["out"] for b in range(8)], axis=0)


# version 06
# speedup vs baseline: 1.2041x; 1.2041x over previous
import math
import sys
import numpy as np
try:
    import concourse.bass as bass
except ImportError:
    sys.path.insert(0, "/opt/trn_rl_repo")
import concourse.bass as bass
import concourse.mybir as mybir
from concourse.bass_utils import run_bass_kernel_spmd

F32 = mybir.dt.float32
BF16 = mybir.dt.bfloat16
I32 = mybir.dt.int32
AF = mybir.ActivationFunctionType
ALU = mybir.AluOpType
AX = mybir.AxisListType

S, D, H, HD, E, FF = 4096, 1024, 16, 64, 8, 2048
NT = S // 128
NC_ = D // 128
NFC = FF // 128
NQB = S // 512
GRP = 512
NGRP = S // GRP

PI = math.pi
HI2PI = 6.28125
LO2PI = 2 * PI - HI2PI
INV2PI = 1.0 / (2 * PI)


class Sem:
    def __init__(self, nc, name):
        self.h = nc.alloc_semaphore(name)
        self.name = name
        self.n = 0


FENCES = ("p0", "tile", "p15", "exp", "qb")


def build_once(dbg, EVp, final):
    nc = bass.Bass("TRN2", target_bir_lowering=False)
    dbg = set(dbg)

    x_h = nc.dram_tensor("x", [S, D], F32, kind="ExternalInput").ap()
    wq_h = nc.dram_tensor("wq", [D, D], F32, kind="ExternalInput").ap()
    wk_h = nc.dram_tensor("wk", [D, D], F32, kind="ExternalInput").ap()
    wv_h = nc.dram_tensor("wv", [D, D], F32, kind="ExternalInput").ap()
    wo_h = nc.dram_tensor("wo", [D, D], F32, kind="ExternalInput").ap()
    wr_h = nc.dram_tensor("wr", [D, E], F32, kind="ExternalInput").ap()
    wg_h = nc.dram_tensor("wg", [E, D, FF], F32, kind="ExternalInput").ap()
    wu_h = nc.dram_tensor("wu", [E, D, FF], F32, kind="ExternalInput").ap()
    wd_h = nc.dram_tensor("wd", [E, FF, D], F32, kind="ExternalInput").ap()
    invf_h = nc.dram_tensor("invf", [128, 32], F32, kind="ExternalInput").ap()
    out_h = nc.dram_tensor("out", [S, D], F32, kind="ExternalOutput").ap()

    def scratch(name, shape, dtype):
        kind = "ExternalOutput" if name in dbg else "Internal"
        return nc.dram_tensor(name, shape, dtype, kind=kind).ap()

    qT_s = scratch("qT_s", [NT, 128, 1024], BF16)
    kT_s = scratch("kT_s", [NT, 128, 1024], BF16)
    v_s = scratch("v_s", [NT, 128, H * 65], BF16)
    g_s = scratch("g_s", [S, D], F32)
    negm_d = scratch("negm_d", [128], F32)

    cur = [20480]
    off = {}

    def carve(name, shape, dtype, at=None):
        size = {F32: 4, BF16: 2, I32: 4}[dtype]
        n_free = int(np.prod(shape[1:]))
        if at is None:
            at = (cur[0] + 31) & ~31
            cur[0] = at + n_free * size
            assert cur[0] <= 229370, (name, cur[0])
        off[name] = at
        return nc.alloc_sbuf_tensor_at(name, list(shape), dtype, offset=at)

    ident = carve("ident", [128, 128], BF16)
    identf = carve("identf", [128, 128], F32)
    masks = carve("masks", [128, 4 * 512], BF16)
    biasT = carve("biasT", [128, 128], F32)
    w_rout = carve("w_rout", [128, NT * E], F32)
    Mbuf = carve("Mbuf", [16, NT], F32)
    kmaxb = carve("kmaxb", [128, NT], F32)
    invfb = carve("invfb", [128, 32], F32)
    ones_f = carve("ones_f", [1, 128], F32)
    posf = carve("posf", [128, NT], F32)
    posi = carve("posi", [128, NT], I32)
    maxk_col = carve("maxk_col", [16, 1], F32)
    negm16 = carve("negm16", [16, 8], F32)
    negm_fl = carve("negm_fl", [1, 128], F32)
    qn2blk = carve("qn2blk", [16, 8], F32)
    smalls = carve("smalls", [128, 16], F32)
    qn2t = carve("qn2t", [128, 16], F32)
    kn2t = carve("kn2t", [128, 16], F32)
    rsm = carve("rsm", [128, 8], F32)
    rsm2 = carve("rsm2", [128, 8], F32)
    rsm3 = carve("rsm3", [128, 8], F32)
    rsm4 = carve("rsm4", [128, 8], F32)
    kk1 = carve("kk1", [32, 1], F32)
    kk2 = carve("kk2", [1, 1], F32)
    angb = carve("angb", [128, 32], F32)
    ubb = carve("ubb", [128, 32], F32)
    nbb = carve("nbb", [128, 32], F32)
    tb2 = carve("tb2", [128, 32], F32)
    PERM_END = (cur[0] + 31) & ~31

    R1 = PERM_END
    xnT = carve("xnT", [128, NC_ * S], BF16, at=R1)
    kTb = carve("kTb", [128, NT * 1024], BF16, at=R1)
    R2 = R1 + 65536

    c1 = [R2]

    def carve1(name, shape, dtype):
        size = {F32: 4, BF16: 2, I32: 4}[dtype]
        at = (c1[0] + 31) & ~31
        c1[0] = at + int(np.prod(shape[1:])) * size
        assert c1[0] <= 229370, (name, c1[0])
        return carve(name, shape, dtype, at=at)

    tabs = carve1("tabs", [128, 4 * 1024], F32)
    wqT = carve1("wqT", [128, NC_ * 1024], BF16)
    wkT = carve1("wkT", [128, NC_ * 1024], BF16)
    wvT = carve1("wvT", [128, NC_ * 1024], BF16)
    xb = carve1("xb", [128, 2 * 1024], F32)
    sqb = carve1("sqb", [128, 1024], F32)
    xnb = carve1("xnb", [128, 1024], BF16)
    qro = carve1("qro", [128, 1024], BF16)
    kro = carve1("kro", [128, 1024], BF16)
    tmA = carve1("tmA", [128, 512], F32)
    tmB = carve1("tmB", [128, 512], F32)
    vsb = carve1("vsb", [128, H * 65], BF16)
    qTt = carve1("qTt", [128, 1024], BF16)
    kTt = carve1("kTt", [128, 1024], BF16)
    wrsb = carve1("wrsb", [128, NC_ * E], BF16)
    stg_w = carve("stg_w", [128, 8192], BF16, at=off["xb"])

    c2 = [R2]

    def carve2(name, shape, dtype):
        size = {F32: 4, BF16: 2}[dtype]
        at = (c2[0] + 31) & ~31
        c2[0] = at + int(np.prod(shape[1:])) * size
        assert c2[0] <= 229370, (name, c2[0])
        return carve(name, shape, dtype, at=at)

    wgsb = carve2("wgsb", [128, NC_ * FF], BF16)
    wusb = carve2("wusb", [128, NC_ * FF], BF16)
    wdsb = carve2("wdsb", [128, NFC * 1024], BF16)
    hT = carve2("hT", [128, NFC * GRP], BF16)
    hsil = carve2("hsil", [128, GRP], F32)
    stg = carve2("stg", [128, 2 * 1024], F32)

    c3 = [R2]

    def carve3(name, shape, dtype):
        size = {F32: 4, BF16: 2}[dtype]
        at = (c3[0] + 31) & ~31
        c3[0] = at + int(np.prod(shape[1:])) * size
        assert c3[0] <= 229370, (name, c3[0])
        return carve(name, shape, dtype, at=at)

    vsb3 = carve3("vsb3", [128, NT * H * 65], BF16)
    qblk = carve3("qblk", [128, 4 * 1024], BF16)
    woT = carve3("woT", [128, NC_ * 1024], BF16)
    Pt = carve3("Pt", [128, 2 * 1024], BF16)
    ontT = carve3("ontT", [128, 2 * NC_ * 512], BF16)
    rdbc = carve3("rdbc", [64, 512], F32)
    lnrow = carve3("lnrow", [1, 512], F32)
    rdrow = carve3("rdrow", [1, 512], F32)
    gb3 = carve3("gb3", [128, 2 * 1024], F32)
    gb3s = carve3("gb3s", [128, 1024], F32)
    stg_wo = carve("stg_wo", [128, 8192], BF16, at=off["gb3"])

    psA = nc.alloc_psum_tensor("psA", [128, 1024], F32)
    ps2 = nc.alloc_psum_tensor("ps2", [128, 1024], F32)
    ps3 = nc.alloc_psum_tensor("ps3", [128, 1024], F32)
    ps4 = nc.alloc_psum_tensor("ps4", [128, 1024], F32)
    psAb = psA.ap().bitcast(BF16)
    ps0b = psAb[:, 0:1024]
    ps1b = psAb[:, 1024:2048]
    ps1 = psA.ap()[:, 512:1024]

    sems = {n: Sem(nc, n) for n in ("sq_x", "sq_s", "sq_w", "sc_v", "sc_s", "sc_t", "sc_g")}
    EV = {}

    def walk(en, holder):
        class _P:
            def __getattr__(self, a):
                return getattr(holder[0], a)
        eng = _P()
        yield
        sq_x, sq_s, sq_w = sems["sq_x"], sems["sq_s"], sems["sq_w"]
        sc_v, sc_s, sc_t, sc_g = sems["sc_v"], sems["sc_s"], sems["sc_t"], sems["sc_g"]

        def ev(key):
            v = EV.get(key)
            if v is None:
                v = EVp.get(key)
            if final:
                assert v is not None, key
            return v

        def W(key):
            v = ev(key) if isinstance(key, str) else key
            if v is not None and v[1] > 0:
                eng.wait_ge(sems[v[0]].h, v[1])

        def I(sem, instr, key=None, k=1):
            instr.then_inc(sem.h, k)
            sem.n += k
            if key is not None:
                EV[key] = (sem.name, sem.n)
            return (sem.name, sem.n)

        def rt(sem, instr, key=None):
            v = I(sem, instr, key)
            eng.wait_ge(sem.h, v[1])
            return v

        def dma(q, sem, out, in_, key=None, **kw):
            q.dma_start(out, in_, **kw).then_inc(sem.h, 16)
            sem.n += 16
            if key is not None:
                EV[key] = (sem.name, sem.n)
            return (sem.name, sem.n)

        led = {"ps0b": None, "ps1b": None}

        if en == "gpsimd":
            g = eng
            rt(sc_g, g.memset(ident[:, :], 1.0))
            g.affine_select(ident[:, :], ident[:, :], [[-1, 128]], ALU.is_equal, 0.0, base=0, channel_multiplier=1)
            rt(sc_g, g.memset(identf[:, :], 1.0))
            g.affine_select(identf[:, :], identf[:, :], [[-1, 128]], ALU.is_equal, 0.0, base=0, channel_multiplier=1)
            rt(sc_g, g.memset(masks[:, :], 1.0))
            for t in range(4):
                rt(sc_g, g.affine_select(masks[:, 512 * t:512 * (t + 1)], masks[:, 512 * t:512 * (t + 1)],
                                         [[1, 512]], ALU.is_ge, 0.0, base=-128 * t, channel_multiplier=-1))
            g.memset(vsb[:, :].rearrange("p (h c) -> p h c", h=H)[:, :, 64:65], 1.0)
            rt(sc_g, g.iota(posi[:, :], [[128, NT]], base=0, channel_multiplier=1))
            I(sc_g, g.tensor_copy(posf[:, :], posi[:, :]), "p0g")
            dma(g, sq_w, wrsb[:, :].rearrange("p (c e) -> p c e", c=NC_),
                wr_h.rearrange("(c p) e -> p c e", p=128), "wr")
        if en == "sync":
            dma(eng, sems["sq_x"], invfb[:, :], invf_h, "invfb")
        if en == "vector":
            I(sc_v, eng.memset(ones_f[:, :], 1.0), "ones")

        for t in range(NT):
            if en == "vector":
                v = eng
                if t == 0:
                    W("invfb"); W("p0g")
                else:
                    W(f"tab{t-1}")
                rt(sc_v, v.tensor_scalar(angb[:, :], invfb[:, :], posf[:, t:t + 1], None, ALU.mult))
                rt(sc_v, v.tensor_scalar(ubb[:, :], angb[:, :], INV2PI, 8388608.0, ALU.mult, ALU.add))
                rt(sc_v, v.tensor_scalar(nbb[:, :], ubb[:, :], -8388608.0, None, ALU.add))
                rt(sc_v, v.scalar_tensor_tensor(tb2[:, :], nbb[:, :], -HI2PI, angb[:, :], ALU.mult, ALU.add))
                rt(sc_v, v.scalar_tensor_tensor(tb2[:, :], nbb[:, :], -LO2PI, tb2[:, :], ALU.mult, ALU.add))
                rt(sc_v, v.tensor_scalar(ubb[:, :], angb[:, :], INV2PI, 0.25, ALU.mult, ALU.add))
                rt(sc_v, v.tensor_scalar(ubb[:, :], ubb[:, :], 8388608.0, None, ALU.add))
                rt(sc_v, v.tensor_scalar(nbb[:, :], ubb[:, :], -8388608.0, None, ALU.add))
                rt(sc_v, v.scalar_tensor_tensor(ubb[:, :], nbb[:, :], -HI2PI, angb[:, :], ALU.mult, ALU.add))
                rt(sc_v, v.scalar_tensor_tensor(ubb[:, :], nbb[:, :], -LO2PI, ubb[:, :], ALU.mult, ALU.add))
                rt(sc_v, v.tensor_scalar(ubb[:, :], ubb[:, :], PI / 2, None, ALU.add), f"ang{t}")
            if en == "scalar":
                s = eng
                W(f"ang{t}")
                s.activation(tabs[:, 1024 + 32 * t:1056 + 32 * t], tb2[:, :], AF.Sin)
                I(sc_s, s.activation(tabs[:, 32 * t:32 * t + 32], ubb[:, :], AF.Sin), f"tab{t}")
        if en == "vector":
            v = eng
            W(f"tab{NT-1}")
            v.tensor_scalar(tabs[:, 2048:3072], tabs[:, 0:1024], 0.125, None, ALU.mult)
            I(sc_v, v.tensor_scalar(tabs[:, 3072:4096], tabs[:, 1024:2048], 0.125, None, ALU.mult), "tabs")

        for i, (wh, wT) in enumerate(((wq_h, wqT), (wk_h, wkT), (wv_h, wvT))):
            if en == "gpsimd":
                if i > 0:
                    W(f"wtre{i-1}_63")
                dma(eng, sems["sq_w"], stg_w[:, :].rearrange("p (c d) -> p c d", c=8),
                    wh.rearrange("(c p) d -> p c d", p=128), f"wst{i}")
            for k in range(64):
                ec, dc = k // 8, k % 8
                bank = "ps0b" if k % 2 == 0 else "ps1b"
                pd = ps0b if k % 2 == 0 else ps1b
                if en == "tensor":
                    if k == 0:
                        W(f"wst{i}")
                        W("p0g")
                    W(led[bank])
                    I(sc_t, eng.matmul(pd[:, 0:128], stg_w[:, ec * 1024 + dc * 128:ec * 1024 + dc * 128 + 128],
                                       ident[:, :], is_transpose=True), f"wtrm{i}_{k}")
                if en == "scalar":
                    W(f"wtrm{i}_{k}")
                    I(sc_s, eng.copy(wT[:, dc * 1024 + ec * 128:dc * 1024 + ec * 128 + 128], pd[:, 0:128]),
                      f"wtre{i}_{k}")
                led[bank] = f"wtre{i}_{k}"

        if en == "sync":
            W("wtre2_63")
            dma(eng, sems["sq_x"], xb[:, 0:1024], x_h[0:128, :], "x0")
        if "p0" in FENCES:
            yield

        for t in range(NT):
            if "tile" in FENCES:
                yield
            xv = xb[:, (t % 2) * 1024:(t % 2) * 1024 + 1024]
            if en == "sync":
                if t + 1 < NT:
                    if t >= 1:
                        W(f"xn{t-1}"); W(f"sq{t-1}")
                    dma(eng, sems["sq_x"], xb[:, ((t + 1) % 2) * 1024:((t + 1) % 2) * 1024 + 1024],
                        x_h[128 * (t + 1):128 * (t + 2), :], f"x{t+1}")
            if en == "scalar":
                s = eng
                W(f"x{t}")
                if t >= 1:
                    W(f"sqfree{t-1}")
                I(sc_s, s.activation(sqb[:, :], xv, AF.Square), f"sq{t}")
            if en == "vector":
                v = eng
                W(f"x{t}")
                v.reduce_sum(smalls[:, 0:1], xv, axis=AX.X)
                W(f"sq{t}")
                rt(sc_v, v.reduce_sum(smalls[:, 1:2], sqb[:, :], axis=AX.X), f"sqfree{t}")
                rt(sc_v, v.tensor_scalar(smalls[:, 0:1], smalls[:, 0:1], 1.0 / D, None, ALU.mult))
                rt(sc_v, v.tensor_scalar(smalls[:, 1:2], smalls[:, 1:2], 1.0 / D, None, ALU.mult))
                rt(sc_v, v.tensor_tensor(smalls[:, 2:3], smalls[:, 0:1], smalls[:, 0:1], ALU.mult))
                rt(sc_v, v.tensor_tensor(smalls[:, 1:2], smalls[:, 1:2], smalls[:, 2:3], ALU.subtract))
                I(sc_v, v.tensor_scalar(smalls[:, 1:2], smalls[:, 1:2], 1e-5, None, ALU.add), f"var{t}")
            if en == "scalar":
                s = eng
                W(f"var{t}")
                rt(sc_s, s.activation(smalls[:, 3:4], smalls[:, 1:2], AF.Ln))
                I(sc_s, s.activation(smalls[:, 3:4], smalls[:, 3:4], AF.Exp, scale=-0.5), f"rstd{t}")
            if en == "vector":
                v = eng
                W(f"rstd{t}")
                if t >= 1:
                    W(f"xnTmm{t-1}")
                I(sc_v, v.tensor_scalar(xnb[:, :], xv, smalls[:, 0:1], smalls[:, 3:4],
                                        ALU.subtract, ALU.mult), f"xn{t}")
            for c in range(NC_):
                bank = "ps0b" if c % 2 == 0 else "ps1b"
                pd = ps0b if c % 2 == 0 else ps1b
                if en == "tensor":
                    te = eng
                    if c == 0:
                        W(f"xn{t}")
                        if t == 0:
                            W("tabs")
                    W(led[bank])
                    key = f"xnTm{t}_{c}"
                    I(sc_t, te.matmul(pd[:, 0:128], xnb[:, c * 128:c * 128 + 128], ident[:, :],
                                      is_transpose=True), key)
                    if c == NC_ - 1:
                        EV[f"xnTmm{t}"] = EV[key]
                if en == "scalar":
                    W(f"xnTm{t}_{c}")
                    I(sc_s, eng.copy(xnT[:, c * S + t * 128:c * S + t * 128 + 128], pd[:, 0:128]), f"xnTe{t}_{c}")
                led[bank] = f"xnTe{t}_{c}"

            if en == "tensor":
                te = eng
                for mi, (psX, wT) in enumerate(((ps2, wqT), (ps3, wkT), (ps4, wvT))):
                    W(f"xnTe{t}_{NC_-1}")
                    if mi == 0 and t == 0:
                        W("wtre2_63")
                    if t >= 1:
                        W([f"rope_q{t-1}", f"rope_k{t-1}", f"wrout{t-1}"][mi])
                    for c in range(NC_):
                        for hf in range(2):
                            last = te.matmul(psX[:, hf * 512:hf * 512 + 512],
                                             xnT[:, c * S + t * 128:c * S + t * 128 + 128],
                                             wT[:, c * 1024 + hf * 512:c * 1024 + hf * 512 + 512],
                                             start=(c == 0), stop=(c == NC_ - 1))
                    I(sc_t, last, f"mm{'qkv'[mi]}{t}")
            if en == "vector":
                v = eng
                for name, ps, rob, c0 in (("q", ps2, qro, 2048), ("k", ps3, kro, 0)):
                    W(f"mm{name}{t}")
                    if t >= 1:
                        W(f"{name}Tmm{t-1}")
                    r_in = ps.ap().rearrange("p (h f two) -> p h f two", h=H, two=2)
                    te_, to_ = r_in[:, :, :, 0:1], r_in[:, :, :, 1:2]
                    r_out = rob[:, :].rearrange("p (h f two) -> p h f two", h=H, two=2)
                    re_, ro_ = r_out[:, :, :, 0:1], r_out[:, :, :, 1:2]
                    cosb = tabs[:, c0 + 32 * t:c0 + 32 * t + 32].unsqueeze(1).broadcast_to([128, H, 32]).unsqueeze(3)
                    sinb = tabs[:, c0 + 1024 + 32 * t:c0 + 1056 + 32 * t].unsqueeze(1).broadcast_to([128, H, 32]).unsqueeze(3)
                    tA = tmA[:, :].rearrange("p (h f) -> p h f", h=H).unsqueeze(3)
                    tB = tmB[:, :].rearrange("p (h f) -> p h f", h=H).unsqueeze(3)
                    v.tensor_tensor(tA, te_, cosb, ALU.mult)
                    rt(sc_v, v.tensor_tensor(tB, to_, sinb, ALU.mult))
                    rt(sc_v, v.tensor_tensor(re_, tA, tB, ALU.subtract))
                    v.tensor_tensor(tA, te_, sinb, ALU.mult)
                    rt(sc_v, v.tensor_tensor(tB, to_, cosb, ALU.mult))
                    rt(sc_v, v.tensor_tensor(ro_, tA, tB, ALU.add), f"rope_{name}{t}")
            if en == "scalar":
                s = eng
                W(f"mmv{t}")
                if t >= 1:
                    W(f"vst{t-1}")
                for h in range(H):
                    last = s.copy(vsb[:, h * 65:h * 65 + 64], ps4[:, h * 64:h * 64 + 64])
                v_ev = I(sc_s, last, f"vev{t}")
                eng.wait_ge(sc_s.h, v_ev[1])
                dma(s, sems["sq_s"], v_s[t], vsb[:, :], f"vst{t}")
            if en == "tensor":
                te = eng
                W(f"vev{t}")
                if t == 0:
                    W("wr")
                for c in range(NC_):
                    last = te.matmul(ps4[:, 0:8], xnT[:, c * S + t * 128:c * S + t * 128 + 128],
                                     wrsb[:, c * E:c * E + E], start=(c == 0), stop=(c == NC_ - 1))
                I(sc_t, last, f"mmr{t}")
            if en == "vector":
                W(f"mmr{t}")
                I(sc_v, eng.reduce_max(smalls[:, 4:5], ps4[:, 0:8], axis=AX.X, negate=True), f"rmax{t}")
            if en == "scalar":
                W(f"rmax{t}")
                I(sc_s, eng.activation(rsm[:, :], ps4[:, 0:8], AF.Exp, bias=smalls[:, 4:5]), f"rexp{t}")
            if en == "vector":
                v = eng
                W(f"rexp{t}")
                rt(sc_v, v.reduce_sum(smalls[:, 5:6], rsm[:, :], axis=AX.X))
                rt(sc_v, v.reciprocal(smalls[:, 6:7], smalls[:, 5:6]))
                rt(sc_v, v.tensor_scalar(rsm[:, :], rsm[:, :], smalls[:, 6:7], None, ALU.mult))
                rt(sc_v, v.reduce_max(smalls[:, 7:8], rsm[:, :], axis=AX.X))
                rt(sc_v, v.tensor_scalar(rsm2[:, :], rsm[:, :], smalls[:, 7:8], None, ALU.is_equal))
                rt(sc_v, v.scalar_tensor_tensor(rsm3[:, :], rsm2[:, :], -1e30, rsm[:, :], ALU.mult, ALU.add))
                I(sc_v, v.reduce_max(smalls[:, 8:9], rsm3[:, :], axis=AX.X), f"rtop{t}")
            if en == "scalar":
                s = eng
                W(f"rtop{t}")
                s.activation(smalls[:, 9:10], smalls[:, 7:8], AF.Exp)
                I(sc_s, s.activation(smalls[:, 10:11], smalls[:, 8:9], AF.Exp), f"rexp2{t}")
            if en == "vector":
                v = eng
                W(f"rexp2{t}")
                rt(sc_v, v.tensor_tensor(smalls[:, 11:12], smalls[:, 9:10], smalls[:, 10:11], ALU.add))
                rt(sc_v, v.reciprocal(smalls[:, 12:13], smalls[:, 11:12]))
                rt(sc_v, v.tensor_scalar(rsm2[:, :], rsm2[:, :], smalls[:, 9:10], smalls[:, 12:13], ALU.mult, ALU.mult))
                rt(sc_v, v.tensor_scalar(rsm4[:, :], rsm3[:, :], smalls[:, 8:9], None, ALU.is_equal))
                rt(sc_v, v.tensor_scalar(rsm4[:, :], rsm4[:, :], smalls[:, 10:11], smalls[:, 12:13], ALU.mult, ALU.mult))
                I(sc_v, v.tensor_tensor(w_rout[:, t * E:t * E + E], rsm2[:, :], rsm4[:, :], ALU.add), f"wrout{t}")
            if en == "scalar":
                W(f"rope_q{t}")
                I(sc_s, eng.activation(sqb[:, :], qro[:, :], AF.Square), f"sqq{t}")
            if en == "vector":
                W(f"sqq{t}")
                I(sc_v, eng.reduce_sum(qn2t[:, :], sqb[:, :].rearrange("p (h f) -> p h f", h=H), axis=AX.X), f"qn2{t}")
            if en == "tensor":
                W(f"qn2{t}")
                W(led["ps1b"])
                I(sc_t, eng.transpose(ps1[0:16, 0:128], qn2t[:, :], identf[:, :]), f"qn2T{t}")
            if en == "vector":
                W(f"qn2T{t}")
                I(sc_v, eng.reduce_max(Mbuf[:, t:t + 1], ps1[0:16, 0:128], axis=AX.X), f"Mb{t}")
            led["ps1b"] = f"Mb{t}"
            if en == "scalar":
                W(f"rope_k{t}")
                W(f"qn2{t}")
                I(sc_s, eng.activation(sqb[:, :], kro[:, :], AF.Square), f"sqk{t}")
            if en == "vector":
                v = eng
                W(f"sqk{t}")
                rt(sc_v, v.reduce_sum(kn2t[:, :], sqb[:, :].rearrange("p (h f) -> p h f", h=H), axis=AX.X))
                I(sc_v, v.reduce_max(kmaxb[:, t:t + 1], kn2t[:, :], axis=AX.X), f"kn2{t}")
            for name, rob, tile, store in (("q", qro, qTt, qT_s), ("k", kro, kTt, kT_s)):
                for c in range(NC_):
                    bank = "ps0b" if c % 2 == 0 else "ps1b"
                    pd = ps0b if c % 2 == 0 else ps1b
                    if en == "tensor":
                        W(led[bank])
                        key = f"{name}Tm{t}_{c}"
                        I(sc_t, eng.matmul(pd[:, 0:128], rob[:, c * 128:c * 128 + 128], ident[:, :],
                                           is_transpose=True), key)
                        if c == NC_ - 1:
                            EV[f"{name}Tmm{t}"] = EV[key]
                    if en == "scalar":
                        W(f"{name}Tm{t}_{c}")
                        if c == 0 and t >= 1:
                            W(f"{name}st{t-1}")
                        I(sc_s, eng.copy(tile[:, c * 128:c * 128 + 128], pd[:, 0:128]), f"{name}Te{t}_{c}")
                    led[bank] = f"{name}Te{t}_{c}"
                if en == "scalar":
                    W(f"{name}Te{t}_{NC_-1}")
                    dma(eng, sems["sq_s"], store[t], tile[:, :], f"{name}st{t}")

        if en == "sync":
            for i_ in range(4):
                dma(eng, sems["sq_x"], out_h[i_ * 1024:(i_ + 1) * 1024, :],
                    x_h[i_ * 1024:(i_ + 1) * 1024, :], "xcopy" if i_ == 3 else None)

        if "p15" in FENCES:
            yield
        if en == "tensor":
            W(f"kn2{NT-1}")
            W(led["ps1b"])
            I(sc_t, eng.transpose(ps1[0:32, 0:128], kmaxb[:, :], identf[:, :]), "km1")
        if en == "vector":
            W("km1")
            I(sc_v, eng.reduce_max(kk1[:, :], ps1[0:32, 0:128], axis=AX.X), "km2")
        if en == "tensor":
            W("km2")
            I(sc_t, eng.transpose(ps1[0:1, 0:32], kk1[:, :], identf[0:32, 0:32]), "km3")
        if en == "vector":
            v = eng
            W("km3")
            I(sc_v, v.reduce_max(kk2[:, :], ps1[0:1, 0:32], axis=AX.X), "km4")
            W(f"Mb{NT-1}")
            I(sc_v, v.reduce_max(qn2blk[:, :], Mbuf[:, :].rearrange("p (b f) -> p b f", b=8), axis=AX.X), "qbm")
        if en == "tensor":
            W("km4")
            I(sc_t, eng.matmul(ps1[0:16, 0:1], ones_f[0:1, 0:16], kk2[0:1, :], start=True, stop=True), "km5")
        if en == "vector":
            v = eng
            W("km5")
            rt(sc_v, v.tensor_copy(maxk_col[:, :], ps1[0:16, 0:1]), "km6")
            W("qbm")
            I(sc_v, v.tensor_scalar(qn2blk[:, :], qn2blk[:, :], maxk_col[:, 0:1], None, ALU.mult), "prod")
        led["ps1b"] = "km6"
        if en == "scalar":
            s = eng
            W("prod")
            rt(sc_s, s.activation(negm16[:, :], qn2blk[:, :], AF.Ln))
            I(sc_s, s.activation(negm16[:, :], negm16[:, :], AF.Exp, scale=0.5), "negm_a")
        if en == "vector":
            W("negm_a")
            I(sc_v, eng.tensor_scalar(negm16[:, :], negm16[:, :], -1.01, None, ALU.mult), "negm")
        if en == "sync":
            sync = eng
            W("negm")
            v1 = dma(sync, sems["sq_x"], negm_d.rearrange("(h e) -> h e", h=16), negm16[:, :])
            sync.wait_ge(sems["sq_x"].h, v1[1])
            dma(sync, sems["sq_x"], negm_fl[0:1, :], negm_d.rearrange("(o f) -> o f", o=1), "negm_fl")
        if en == "tensor":
            W("negm_fl")
            W(led["ps1b"])
            I(sc_t, eng.matmul(ps1[:, 0:128], ones_f[0:1, :], negm_fl[0:1, :], start=True, stop=True), "biasT_mm")
        if en == "vector":
            W("biasT_mm")
            I(sc_v, eng.tensor_copy(biasT[:, :], ps1[:, 0:128]), "biasT")
        led["ps1b"] = "biasT"

        g_acc_ev = [None] * NT
        last_sil = None
        last_hm = None
        last_dn = None
        last_stg = {0: None, 1: None}
        last_gacc_stg = {0: None, 1: None}
        for e in range(E):
            if "exp" in FENCES:
                yield
            if en == "gpsimd":
                g = eng
                if e == 0:
                    W("biasT")
                    W(f"kst{NT-1}")
                else:
                    W(f"gu_done{e-1}")
                dma(g, sems["sq_w"], wgsb[:, :].rearrange("p (c f) -> p c f", c=NC_),
                    wg_h[e].rearrange("(c p) f -> p c f", p=128))
                dma(g, sems["sq_w"], wusb[:, :].rearrange("p (c f) -> p c f", c=NC_),
                    wu_h[e].rearrange("(c p) f -> p c f", p=128), f"wgu{e}")
                if e >= 1:
                    W(f"dn_done{e-1}")
                dma(g, sems["sq_w"], wdsb[:, :].rearrange("p (f d) -> p f d", f=NFC),
                    wd_h[e].rearrange("(f p) d -> p f d", p=128), f"wd{e}")
            for grp in range(NGRP):
                t0 = grp * GRP
                for fc in range(NFC):
                    if en == "tensor":
                        te = eng
                        if fc == 0 and grp == 0:
                            W(f"wgu{e}")
                            if e == 0:
                                W(f"xnTe{NT-1}_{NC_-1}")
                                W(f"wrout{NT-1}")
                        W(last_sil)
                        for c in range(NC_):
                            te.matmul(ps4[:, 0:512], wgsb[:, c * FF + fc * 128:c * FF + fc * 128 + 128],
                                      xnT[:, c * S + t0:c * S + t0 + GRP], start=(c == 0), stop=(c == NC_ - 1))
                        W(last_hm)
                        for c in range(NC_):
                            last = te.matmul(ps4[:, 512:1024], wusb[:, c * FF + fc * 128:c * FF + fc * 128 + 128],
                                             xnT[:, c * S + t0:c * S + t0 + GRP], start=(c == 0), stop=(c == NC_ - 1))
                        key = f"gu{e}_{grp}_{fc}"
                        I(sc_t, last, key)
                        if grp == NGRP - 1 and fc == NFC - 1:
                            EV[f"gu_done{e}"] = EV[key]
                    if en == "scalar":
                        W(f"gu{e}_{grp}_{fc}")
                        I(sc_s, eng.activation(hsil[:, :], ps4[:, 0:512], AF.Silu), f"sil{e}_{grp}_{fc}")
                    last_sil = f"sil{e}_{grp}_{fc}"
                    if en == "vector":
                        W(f"sil{e}_{grp}_{fc}")
                        W(last_dn)
                        I(sc_v, eng.tensor_tensor(hT[:, fc * GRP:fc * GRP + GRP], ps4[:, 512:1024], hsil[:, :],
                                                  ALU.mult), f"hm{e}_{grp}_{fc}")
                    last_hm = f"hm{e}_{grp}_{fc}"
                for ti in range(4):
                    tt = grp * 4 + ti
                    par = ti % 2
                    psD = ps3 if par == 0 else ps2
                    if en == "tensor":
                        te = eng
                        if ti == 0:
                            W(f"hm{e}_{grp}_{NFC-1}")
                            if grp == 0:
                                W(f"wd{e}")
                                if e == 0:
                                    W(f"rope_q{NT-1}")
                                    W(f"rope_k{NT-1}")
                        W(last_stg[par])
                        for fc in range(NFC):
                            for hf in range(2):
                                last = te.matmul(psD[:, hf * 512:hf * 512 + 512],
                                                 hT[:, fc * GRP + ti * 128:fc * GRP + ti * 128 + 128],
                                                 wdsb[:, fc * 1024 + hf * 512:fc * 1024 + hf * 512 + 512],
                                                 start=(fc == 0), stop=(fc == NFC - 1))
                        key = f"dnm{e}_{tt}"
                        I(sc_t, last, key)
                        if ti == 3:
                            EV[f"dn{e}_{grp}"] = EV[key]
                            if grp == NGRP - 1:
                                EV[f"dn_done{e}"] = EV[key]
                    if ti == 3:
                        last_dn = f"dn{e}_{grp}"
                    if en == "vector":
                        W(f"dnm{e}_{tt}")
                        W(last_gacc_stg[tt % 2])
                        I(sc_v, eng.tensor_scalar(stg[:, (tt % 2) * 1024:(tt % 2) * 1024 + 1024], psD[:, :],
                                                  w_rout[:, tt * E + e:tt * E + e + 1], None, ALU.mult),
                          f"stg{e}_{tt}")
                    last_stg[par] = f"stg{e}_{tt}"
                    if en == "gpsimd":
                        g = eng
                        W(f"stg{e}_{tt}")
                        W(g_acc_ev[tt])
                        kw = dict(accum_op=ALU.add) if e >= 1 else {}
                        dma(g, sems["sq_w"], g_s[tt * 128:tt * 128 + 128, :],
                            stg[:, (tt % 2) * 1024:(tt % 2) * 1024 + 1024], f"gacc{e}_{tt}", **kw)
                    g_acc_ev[tt] = f"gacc{e}_{tt}"
                    last_gacc_stg[tt % 2] = f"gacc{e}_{tt}"

        P2_T = f"dn_done{E-1}"
        P2_V = f"stg{E-1}_{NT-1}"
        P2_G = f"gacc{E-1}_{NT-1}"

        if en == "gpsimd":
            g = eng
            W(P2_T); W(P2_V)
            dma(g, sems["sq_w"], stg_wo[:, :].rearrange("p (c d) -> p c d", c=8),
                wo_h.rearrange("(c p) d -> p c d", p=128), "wo_st")
        if en == "sync":
            sync = eng
            W(P2_T)
            for t in range(NT):
                dma(sync, sems["sq_x"], kTb[:, t * 1024:t * 1024 + 1024], kT_s[t],
                    "kT_ld" if t == NT - 1 else None)
        if en == "scalar":
            s = eng
            W(P2_T); W(P2_V); W(P2_G)
            for t in range(NT):
                dma(s, sems["sq_s"], vsb3[:, t * H * 65:(t + 1) * H * 65], v_s[t],
                    "v_ld" if t == NT - 1 else None)
        for k in range(64):
            ec, dc = k // 8, k % 8
            bank = "ps0b" if k % 2 == 0 else "ps1b"
            pd = ps0b if k % 2 == 0 else ps1b
            if en == "tensor":
                if k == 0:
                    W("wo_st")
                W(led[bank])
                I(sc_t, eng.matmul(pd[:, 0:128], stg_wo[:, ec * 1024 + dc * 128:ec * 1024 + dc * 128 + 128],
                                   ident[:, :], is_transpose=True), f"woTm{k}")
            if en == "scalar":
                W(f"woTm{k}")
                I(sc_s, eng.copy(woT[:, dc * 1024 + ec * 128:dc * 1024 + ec * 128 + 128], pd[:, 0:128]),
                  f"woTe{k}")
            led[bank] = f"woTe{k}"

        def emit_wo_tile(qbw, ti):
            tt = 4 * qbw + ti
            if en == "sync":
                if qbw == 0 and ti == 0:
                    W(P2_G)
                for pf in ([tt, tt + 1] if ti == 0 else ([tt + 1] if ti < 3 else [])):
                    if 4 * qbw <= pf < 4 * qbw + 4:
                        if pf >= 2:
                            W(f"ost{pf-2}")
                        dma(eng, sems["sq_x"], gb3[:, (pf % 2) * 1024:(pf % 2) * 1024 + 1024],
                            g_s[pf * 128:pf * 128 + 128, :], f"g_ld{pf}")
            if en == "scalar":
                W(f"g_ld{tt}")
                if tt >= 1:
                    W(f"comb{tt-1}")
                I(sc_s, eng.activation(gb3s[:, :], gb3[:, (tt % 2) * 1024:(tt % 2) * 1024 + 1024],
                                       AF.Silu), f"gsil{tt}")
            if en == "tensor":
                te = eng
                W(f"ont{qbw}_{H-1}")
                if tt == 0:
                    W("woTe63")
                if tt >= 1:
                    W(f"comb{tt-1}")
                for dc in range(NC_):
                    for hf in range(2):
                        last = te.matmul(ps4[:, hf * 512:hf * 512 + 512],
                                         ontT[:, (qbw % 2) * 4096 + dc * 512 + ti * 128:(qbw % 2) * 4096 + dc * 512 + ti * 128 + 128],
                                         woT[:, dc * 1024 + hf * 512:dc * 1024 + hf * 512 + 512],
                                         start=(dc == 0), stop=(dc == NC_ - 1))
                I(sc_t, last, f"womm{tt}")
            if en == "vector":
                v = eng
                W(f"womm{tt}")
                W(f"gsil{tt}")
                if tt >= 1:
                    W(f"ost{tt-1}")
                I(sc_v, v.tensor_tensor(gb3[:, (tt % 2) * 1024:(tt % 2) * 1024 + 1024],
                                        ps4[:, :], gb3s[:, :], ALU.mult), f"comb{tt}")
            if en == "gpsimd":
                g = eng
                W(f"comb{tt}")
                if tt == 0:
                    W("xcopy")
                dma(g, sems["sq_w"], out_h[tt * 128:tt * 128 + 128, :],
                    gb3[:, (tt % 2) * 1024:(tt % 2) * 1024 + 1024], f"ost{tt}", accum_op=ALU.add)

        pair_i = 0
        for qb in range(NQB):
            if "qb" in FENCES:
                yield
            if en == "sync":
                sync = eng
                if qb == 0:
                    W(P2_T)
                else:
                    W(f"sc_last{qb-1}")
                dma(sync, sems["sq_x"], qblk[:, :].rearrange("p (t c) -> p t c", t=4),
                    qT_s[4 * qb:4 * qb + 4].transpose([1, 0, 2]), f"q_ld{qb}")
            npair = 2 * qb + 2
            for h in range(H):
                if qb >= 1 and 1 <= h <= 4:
                    emit_wo_tile(qb - 1, h - 1)
                hp, hr = h // 2, h % 2
                cps0 = pair_i
                for idx in range(npair + 1):
                    if idx < npair:
                        pp = idx
                        cp = cps0 + pp
                        c0, c1_ = 2 * pp, 2 * pp + 1
                        psS = psA.ap() if cp % 2 == 0 else ps2.ap()
                        pth = Pt[:, (cp % 2) * 1024:(cp % 2) * 1024 + 1024]
                        if en == "tensor":
                            te = eng
                            if h == 0 and pp == 0:
                                W("kT_ld")
                                W(f"q_ld{qb}")
                                if qb == 0:
                                    W("biasT")
                                    W(P2_V)
                            if cp >= 2:
                                W(f"expP{cp-2}")
                            rq = qblk[hr * 64:hr * 64 + 64, :].rearrange("p (t c2) -> p t c2", t=4)[:, :, hp * 128:hp * 128 + 128]
                            te.matmul(psS[:, 0:512], kTb[hr * 64:hr * 64 + 64, c0 * 1024 + hp * 128:c0 * 1024 + hp * 128 + 128],
                                      rq, start=True, stop=True)
                            I(sc_t, te.matmul(psS[:, 512:1024], kTb[hr * 64:hr * 64 + 64, c1_ * 1024 + hp * 128:c1_ * 1024 + hp * 128 + 128],
                                              rq, start=True, stop=True), f"smmP{cp}")
                        if en == "scalar":
                            W(f"smmP{cp}")
                            if cp >= 2:
                                W(f"avP{cp-2}")
                            I(sc_s, eng.activation(pth, psS[:, :], AF.Exp,
                                                   bias=biasT[:, h * 8 + qb:h * 8 + qb + 1]), f"expP{cp}")
                        if c0 >= 4 * qb and en == "vector":
                            W(f"expP{cp}")
                            tloc = c0 - 4 * qb
                            I(sc_v, eng.tensor_tensor(pth, pth, masks[:, tloc * 512:tloc * 512 + 1024], ALU.mult),
                              f"mskP{cp}")
                    if idx >= 1:
                        pp = idx - 1
                        cp = cps0 + pp
                        c0, c1_ = 2 * pp, 2 * pp + 1
                        boundary = c0 >= 4 * qb
                        pth = Pt[:, (cp % 2) * 1024:(cp % 2) * 1024 + 1024]
                        if en == "tensor":
                            te = eng
                            W(f"mskP{cp}" if boundary else f"expP{cp}")
                            if pp == 0:
                                if h >= 2:
                                    W(f"ont{qb}_{h-2}")
                                elif qb >= 1:
                                    W(f"ont{qb-1}_{H-2+h}")
                                if h == 0 and qb == 0:
                                    W("v_ld")
                            psV = ps3.ap()[0:65, (h % 2) * 512:(h % 2) * 512 + 512]
                            te.matmul(psV, vsb3[:, c0 * H * 65 + h * 65:c0 * H * 65 + h * 65 + 65],
                                      pth[:, 0:512], start=(c0 == 0), stop=False)
                            key = f"avP{cp}"
                            I(sc_t, te.matmul(psV, vsb3[:, c1_ * H * 65 + h * 65:c1_ * H * 65 + h * 65 + 65],
                                              pth[:, 512:1024], start=False, stop=(c1_ == 4 * qb + 3)), key)
                            if c1_ == 4 * qb + 3:
                                EV[f"av{qb}_{h}"] = EV[key]
                                if h == H - 1:
                                    EV[f"sc_last{qb}"] = EV[key]
                pair_i += npair
                if en == "scalar":
                    s = eng
                    W(f"av{qb}_{h}")
                    psF = ps3.ap()
                    rt(sc_s, s.activation(lnrow[0:1, :], psF[64:65, (h % 2) * 512:(h % 2) * 512 + 512], AF.Ln))
                    I(sc_s, s.activation(rdrow[0:1, :], lnrow[0:1, :], AF.Exp, scale=-1.0), f"rd{qb}_{h}")
                if en == "tensor":
                    W(f"rd{qb}_{h}")
                    if qb == 0 and h < 2:
                        W(led["ps1b" if h % 2 else "ps0b"])
                    I(sc_t, eng.matmul(ps3.ap()[64:128, (h % 2) * 512:(h % 2) * 512 + 512],
                                       ones_f[0:1, 0:64], rdrow[0:1, :], start=True, stop=True),
                      f"rdbcm{qb}_{h}")
                if en == "vector":
                    v = eng
                    W(f"rdbcm{qb}_{h}")
                    if h == 0 and qb >= 2:
                        W(f"womm{4*(qb-1)-1}")
                    rt(sc_v, v.tensor_copy(rdbc[:, :], ps3.ap()[64:128, (h % 2) * 512:(h % 2) * 512 + 512]),
                       f"rdbce{qb}_{h}")
                    I(sc_v, v.tensor_tensor(ontT[hr * 64:hr * 64 + 64, (qb % 2) * 4096 + hp * 512:(qb % 2) * 4096 + hp * 512 + 512],
                                            ps3[0:64, (h % 2) * 512:(h % 2) * 512 + 512],
                                            rdbc[0:64, :], ALU.mult), f"ont{qb}_{h}")
            if qb == NQB - 1:
                for ti in range(4):
                    emit_wo_tile(qb, ti)

        if en == "scalar":
            EV["tot_sq_s"] = ("sq_s", sems["sq_s"].n)
        if en == "gpsimd":
            EV["tot_sq_w"] = ("sq_w", sems["sq_w"].n)
        if en == "sync":
            eng.wait_ge(sems["sq_x"].h, sems["sq_x"].n)
            W("tot_sq_s")
            W("tot_sq_w")

    n_fence = 1 + (1 if "p0" in FENCES else 0) + (NT if "tile" in FENCES else 0) \
        + (1 if "p15" in FENCES else 0) + (E if "exp" in FENCES else 0) + (NQB if "qb" in FENCES else 0)
    ENG = ("sync", "gpsimd", "scalar", "vector", "tensor")
    holders = {en: [None] for en in ENG}
    gens = {en: walk(en, holders[en]) for en in ENG}
    for en in ENG:
        next(gens[en])
    with nc.Block() as blk:
        for seg in range(n_fence):
            for en in ENG:
                def run(engobj, en=en):
                    holders[en][0] = engobj
                    try:
                        next(gens[en])
                    except StopIteration:
                        pass
                getattr(blk, en)(run)
    return nc, EV


def build(dbg=()):
    global EVp
    EVp = {}
    _, ev1 = build_once(dbg, EVp, final=False)
    EVp = ev1
    nc, ev2 = build_once(dbg, EVp, final=True)
    assert ev1 == ev2, "event mismatch between passes"
    return nc


EVp = {}
_NC_CACHE = {}


def kernel(**inputs):
    if "nc" not in _NC_CACHE:
        _NC_CACHE["nc"] = build()
    nc = _NC_CACHE["nc"]
    x = np.ascontiguousarray(inputs["L0_attn_x"])
    base = {
        "wq": np.ascontiguousarray(inputs["L0_attn_Wq"]),
        "wk": np.ascontiguousarray(inputs["L0_attn_Wk"]),
        "wv": np.ascontiguousarray(inputs["L0_attn_Wv"]),
        "wo": np.ascontiguousarray(inputs["L0_attn_Wo"]),
        "wr": np.ascontiguousarray(inputs["L0_ffn_kernel_router_DE"]),
        "wg": np.ascontiguousarray(inputs["L0_ffn_kernel_gating_EDF"]),
        "wu": np.ascontiguousarray(inputs["L0_ffn_kernel_up_EDF"]),
        "wd": np.ascontiguousarray(inputs["L0_ffn_kernel_down_EFD"]),
        "invf": np.ascontiguousarray(np.broadcast_to(
            (10000.0 ** (-np.arange(0, 64, 2, dtype=np.float64) / 64.0)).astype(np.float32)[None, :],
            (128, 32))),
    }
    in_maps = [dict(base, x=np.ascontiguousarray(x[b])) for b in range(8)]
    res = run_bass_kernel_spmd(nc, in_maps, core_ids=list(range(8)))
    return np.stack([res.results[b]["out"] for b in range(8)], axis=0)


# version 07
# speedup vs baseline: 1.3222x; 1.0980x over previous
import math
import sys
import numpy as np
try:
    import concourse.bass as bass
except ImportError:
    sys.path.insert(0, "/opt/trn_rl_repo")
import concourse.bass as bass
import concourse.mybir as mybir
from concourse.bass_utils import run_bass_kernel_spmd

F32 = mybir.dt.float32
BF16 = mybir.dt.bfloat16
I32 = mybir.dt.int32
AF = mybir.ActivationFunctionType
ALU = mybir.AluOpType
AX = mybir.AxisListType

S, D, H, HD, E, FF = 4096, 1024, 16, 64, 8, 2048
NT = S // 128
NC_ = D // 128
NFC = FF // 128
NQB = S // 512
GRP = 512
NGRP = S // GRP

PI = math.pi
HI2PI = 6.28125
LO2PI = 2 * PI - HI2PI
INV2PI = 1.0 / (2 * PI)


class Sem:
    def __init__(self, nc, name):
        self.h = nc.alloc_semaphore(name)
        self.name = name
        self.n = 0


FENCES = ("p0", "tile", "p15", "exp", "qb")


def build_once(dbg, EVp, final):
    nc = bass.Bass("TRN2", target_bir_lowering=False)
    dbg = set(dbg)

    x_h = nc.dram_tensor("x", [S, D], F32, kind="ExternalInput").ap()
    wq_h = nc.dram_tensor("wq", [D, D], F32, kind="ExternalInput").ap()
    wk_h = nc.dram_tensor("wk", [D, D], F32, kind="ExternalInput").ap()
    wv_h = nc.dram_tensor("wv", [D, D], F32, kind="ExternalInput").ap()
    wo_h = nc.dram_tensor("wo", [D, D], F32, kind="ExternalInput").ap()
    wr_h = nc.dram_tensor("wr", [D, E], F32, kind="ExternalInput").ap()
    wg_h = nc.dram_tensor("wg", [E, D, FF], F32, kind="ExternalInput").ap()
    wu_h = nc.dram_tensor("wu", [E, D, FF], F32, kind="ExternalInput").ap()
    wd_h = nc.dram_tensor("wd", [E, FF, D], F32, kind="ExternalInput").ap()
    invf_h = nc.dram_tensor("invf", [128, 32], F32, kind="ExternalInput").ap()
    out_h = nc.dram_tensor("out", [S, D], F32, kind="ExternalOutput").ap()

    def scratch(name, shape, dtype):
        kind = "ExternalOutput" if name in dbg else "Internal"
        return nc.dram_tensor(name, shape, dtype, kind=kind).ap()

    qT_s = scratch("qT_s", [NT, 128, 1024], BF16)
    kT_s = scratch("kT_s", [NT, 128, 1024], BF16)
    v_s = scratch("v_s", [NT, 128, H * 65], BF16)
    g_s = scratch("g_s", [S, D], F32)
    negm_d = scratch("negm_d", [128], F32)

    cur = [20480]
    off = {}

    def carve(name, shape, dtype, at=None):
        size = {F32: 4, BF16: 2, I32: 4}[dtype]
        n_free = int(np.prod(shape[1:]))
        if at is None:
            at = (cur[0] + 31) & ~31
            cur[0] = at + n_free * size
            assert cur[0] <= 229370, (name, cur[0])
        off[name] = at
        return nc.alloc_sbuf_tensor_at(name, list(shape), dtype, offset=at)

    ident = carve("ident", [128, 128], BF16)
    identf = carve("identf", [128, 128], F32)
    masks = carve("masks", [128, 4 * 512], BF16)
    biasT = carve("biasT", [128, 128], F32)
    w_rout = carve("w_rout", [128, NT * E], F32)
    Mbuf = carve("Mbuf", [16, NT], F32)
    kmaxb = carve("kmaxb", [128, NT], F32)
    invfb = carve("invfb", [128, 32], F32)
    ones_f = carve("ones_f", [1, 128], F32)
    posf = carve("posf", [128, NT], F32)
    posi = carve("posi", [128, NT], I32)
    maxk_col = carve("maxk_col", [16, 1], F32)
    negm16 = carve("negm16", [16, 8], F32)
    negm_fl = carve("negm_fl", [1, 128], F32)
    qn2blk = carve("qn2blk", [16, 8], F32)
    smalls = carve("smalls", [128, 16], F32)
    qn2t = carve("qn2t", [128, 16], F32)
    kn2t = carve("kn2t", [128, 16], F32)
    rsm = carve("rsm", [128, 8], F32)
    rsm2 = carve("rsm2", [128, 8], F32)
    rsm3 = carve("rsm3", [128, 8], F32)
    rsm4 = carve("rsm4", [128, 8], F32)
    kk1 = carve("kk1", [32, 1], F32)
    kk2 = carve("kk2", [1, 1], F32)
    angb = carve("angb", [128, 32], F32)
    ubb = carve("ubb", [128, 32], F32)
    nbb = carve("nbb", [128, 32], F32)
    tb2 = carve("tb2", [128, 32], F32)
    PERM_END = (cur[0] + 31) & ~31

    R1 = PERM_END
    xnT = carve("xnT", [128, NC_ * S], BF16, at=R1)
    kTb = carve("kTb", [128, NT * 1024], BF16, at=R1)
    R2 = R1 + 65536

    c1 = [R2]

    def carve1(name, shape, dtype):
        size = {F32: 4, BF16: 2, I32: 4}[dtype]
        at = (c1[0] + 31) & ~31
        c1[0] = at + int(np.prod(shape[1:])) * size
        assert c1[0] <= 229370, (name, c1[0])
        return carve(name, shape, dtype, at=at)

    tabs = carve1("tabs", [128, 4 * 1024], F32)
    wqT = carve1("wqT", [128, NC_ * 1024], BF16)
    wkT = carve1("wkT", [128, NC_ * 1024], BF16)
    wvT = carve1("wvT", [128, NC_ * 1024], BF16)
    xb = carve1("xb", [128, 2 * 1024], F32)
    sqb = carve1("sqb", [128, 1024], F32)
    xnb = carve1("xnb", [128, 1024], BF16)
    qro = carve1("qro", [128, 1024], BF16)
    kro = carve1("kro", [128, 1024], BF16)
    tmA = carve1("tmA", [128, 512], F32)
    tmB = carve1("tmB", [128, 512], F32)
    vsb = carve1("vsb", [128, H * 65], BF16)
    qTt = carve1("qTt", [128, 1024], BF16)
    kTt = carve1("kTt", [128, 1024], BF16)
    wrsb = carve1("wrsb", [128, NC_ * E], BF16)
    stg_w = carve("stg_w", [128, 8192], BF16, at=off["xb"])

    c2 = [R2]

    def carve2(name, shape, dtype):
        size = {F32: 4, BF16: 2}[dtype]
        at = (c2[0] + 31) & ~31
        c2[0] = at + int(np.prod(shape[1:])) * size
        assert c2[0] <= 229370, (name, c2[0])
        return carve(name, shape, dtype, at=at)

    wgsb = carve2("wgsb", [128, NC_ * FF], BF16)
    wusb = carve2("wusb", [128, NC_ * FF], BF16)
    wdsb = carve2("wdsb", [128, NFC * 1024], BF16)
    hT = carve2("hT", [128, NFC * GRP], BF16)
    hsil = carve2("hsil", [128, 2 * GRP], F32)
    stg = carve2("stg", [128, 2 * 1024], F32)

    c3 = [R2]

    def carve3(name, shape, dtype):
        size = {F32: 4, BF16: 2}[dtype]
        at = (c3[0] + 31) & ~31
        c3[0] = at + int(np.prod(shape[1:])) * size
        assert c3[0] <= 229370, (name, c3[0])
        return carve(name, shape, dtype, at=at)

    vsb3 = carve3("vsb3", [128, NT * H * 65], BF16)
    qblk = carve3("qblk", [128, 4 * 1024], BF16)
    woT = carve3("woT", [128, NC_ * 1024], BF16)
    Pt = carve3("Pt", [128, 2 * 1024], BF16)
    ontT = carve3("ontT", [128, 2 * NC_ * 512], BF16)
    rdbc = carve3("rdbc", [64, 512], F32)
    lnrow = carve3("lnrow", [1, 512], F32)
    rdrow = carve3("rdrow", [1, 512], F32)
    gb3 = carve3("gb3", [128, 2 * 1024], F32)
    gb3s = carve3("gb3s", [128, 1024], F32)
    stg_wo = carve("stg_wo", [128, 8192], BF16, at=off["gb3"])

    psA = nc.alloc_psum_tensor("psA", [128, 1024], F32)
    ps2 = nc.alloc_psum_tensor("ps2", [128, 1024], F32)
    ps3 = nc.alloc_psum_tensor("ps3", [128, 1024], F32)
    ps4 = nc.alloc_psum_tensor("ps4", [128, 1024], F32)
    psAb = psA.ap().bitcast(BF16)
    ps0b = psAb[:, 0:1024]
    ps1b = psAb[:, 1024:2048]
    ps1 = psA.ap()[:, 512:1024]

    sems = {n: Sem(nc, n) for n in ("sq_x", "sq_s", "sq_w", "sc_v", "sc_s", "sc_t", "sc_g")}
    EV = {}

    def walk(en, holder):
        class _P:
            def __getattr__(self, a):
                return getattr(holder[0], a)
        eng = _P()
        yield
        sq_x, sq_s, sq_w = sems["sq_x"], sems["sq_s"], sems["sq_w"]
        sc_v, sc_s, sc_t, sc_g = sems["sc_v"], sems["sc_s"], sems["sc_t"], sems["sc_g"]

        def ev(key):
            v = EV.get(key)
            if v is None:
                v = EVp.get(key)
            if final:
                assert v is not None, key
            return v

        def W(key):
            v = ev(key) if isinstance(key, str) else key
            if v is not None and v[1] > 0:
                eng.wait_ge(sems[v[0]].h, v[1])

        def I(sem, instr, key=None, k=1):
            instr.then_inc(sem.h, k)
            sem.n += k
            if key is not None:
                EV[key] = (sem.name, sem.n)
            return (sem.name, sem.n)

        def rt(sem, instr, key=None):
            v = I(sem, instr, key)
            eng.wait_ge(sem.h, v[1])
            return v

        def dma(q, sem, out, in_, key=None, **kw):
            q.dma_start(out, in_, **kw).then_inc(sem.h, 16)
            sem.n += 16
            if key is not None:
                EV[key] = (sem.name, sem.n)
            return (sem.name, sem.n)

        led = {"ps0b": None, "ps1b": None}

        if en == "gpsimd":
            g = eng
            rt(sc_g, g.memset(ident[:, :], 1.0))
            g.affine_select(ident[:, :], ident[:, :], [[-1, 128]], ALU.is_equal, 0.0, base=0, channel_multiplier=1)
            rt(sc_g, g.memset(identf[:, :], 1.0))
            g.affine_select(identf[:, :], identf[:, :], [[-1, 128]], ALU.is_equal, 0.0, base=0, channel_multiplier=1)
            rt(sc_g, g.memset(masks[:, :], 1.0))
            for t in range(4):
                rt(sc_g, g.affine_select(masks[:, 512 * t:512 * (t + 1)], masks[:, 512 * t:512 * (t + 1)],
                                         [[1, 512]], ALU.is_ge, 0.0, base=-128 * t, channel_multiplier=-1))
            g.memset(vsb[:, :].rearrange("p (h c) -> p h c", h=H)[:, :, 64:65], 1.0)
            rt(sc_g, g.iota(posi[:, :], [[128, NT]], base=0, channel_multiplier=1))
            I(sc_g, g.tensor_copy(posf[:, :], posi[:, :]), "p0g")
            dma(g, sq_w, wrsb[:, :].rearrange("p (c e) -> p c e", c=NC_),
                wr_h.rearrange("(c p) e -> p c e", p=128), "wr")
        if en == "sync":
            dma(eng, sems["sq_x"], invfb[:, :], invf_h, "invfb")
        if en == "vector":
            I(sc_v, eng.memset(ones_f[:, :], 1.0), "ones")

        for t in range(NT):
            if en == "vector":
                v = eng
                if t == 0:
                    W("invfb"); W("p0g")
                else:
                    W(f"tab{t-1}")
                rt(sc_v, v.tensor_scalar(angb[:, :], invfb[:, :], posf[:, t:t + 1], None, ALU.mult))
                rt(sc_v, v.tensor_scalar(ubb[:, :], angb[:, :], INV2PI, 8388608.0, ALU.mult, ALU.add))
                rt(sc_v, v.tensor_scalar(nbb[:, :], ubb[:, :], -8388608.0, None, ALU.add))
                rt(sc_v, v.scalar_tensor_tensor(tb2[:, :], nbb[:, :], -HI2PI, angb[:, :], ALU.mult, ALU.add))
                rt(sc_v, v.scalar_tensor_tensor(tb2[:, :], nbb[:, :], -LO2PI, tb2[:, :], ALU.mult, ALU.add))
                rt(sc_v, v.tensor_scalar(ubb[:, :], angb[:, :], INV2PI, 0.25, ALU.mult, ALU.add))
                rt(sc_v, v.tensor_scalar(ubb[:, :], ubb[:, :], 8388608.0, None, ALU.add))
                rt(sc_v, v.tensor_scalar(nbb[:, :], ubb[:, :], -8388608.0, None, ALU.add))
                rt(sc_v, v.scalar_tensor_tensor(ubb[:, :], nbb[:, :], -HI2PI, angb[:, :], ALU.mult, ALU.add))
                rt(sc_v, v.scalar_tensor_tensor(ubb[:, :], nbb[:, :], -LO2PI, ubb[:, :], ALU.mult, ALU.add))
                rt(sc_v, v.tensor_scalar(ubb[:, :], ubb[:, :], PI / 2, None, ALU.add), f"ang{t}")
            if en == "scalar":
                s = eng
                W(f"ang{t}")
                s.activation(tabs[:, 1024 + 32 * t:1056 + 32 * t], tb2[:, :], AF.Sin)
                I(sc_s, s.activation(tabs[:, 32 * t:32 * t + 32], ubb[:, :], AF.Sin), f"tab{t}")
        if en == "vector":
            v = eng
            W(f"tab{NT-1}")
            v.tensor_scalar(tabs[:, 2048:3072], tabs[:, 0:1024], 0.125, None, ALU.mult)
            I(sc_v, v.tensor_scalar(tabs[:, 3072:4096], tabs[:, 1024:2048], 0.125, None, ALU.mult), "tabs")

        for i, (wh, wT) in enumerate(((wq_h, wqT), (wk_h, wkT), (wv_h, wvT))):
            if en == "gpsimd":
                if i > 0:
                    W(f"wtre{i-1}_63")
                dma(eng, sems["sq_w"], stg_w[:, :].rearrange("p (c d) -> p c d", c=8),
                    wh.rearrange("(c p) d -> p c d", p=128), f"wst{i}")
            for k in range(64):
                ec, dc = k // 8, k % 8
                bank = "ps0b" if k % 2 == 0 else "ps1b"
                pd = ps0b if k % 2 == 0 else ps1b
                if en == "tensor":
                    if k == 0:
                        W(f"wst{i}")
                        W("p0g")
                    W(led[bank])
                    I(sc_t, eng.matmul(pd[:, 0:128], stg_w[:, ec * 1024 + dc * 128:ec * 1024 + dc * 128 + 128],
                                       ident[:, :], is_transpose=True), f"wtrm{i}_{k}")
                if en == "scalar":
                    W(f"wtrm{i}_{k}")
                    I(sc_s, eng.copy(wT[:, dc * 1024 + ec * 128:dc * 1024 + ec * 128 + 128], pd[:, 0:128]),
                      f"wtre{i}_{k}")
                led[bank] = f"wtre{i}_{k}"

        if en == "sync":
            W("wtre2_63")
            dma(eng, sems["sq_x"], xb[:, 0:1024], x_h[0:128, :], "x0")
        if "p0" in FENCES:
            yield

        for t in range(NT):
            if "tile" in FENCES:
                yield
            xv = xb[:, (t % 2) * 1024:(t % 2) * 1024 + 1024]
            if en == "sync":
                if t + 1 < NT:
                    if t >= 1:
                        W(f"xn{t-1}"); W(f"sq{t-1}")
                    dma(eng, sems["sq_x"], xb[:, ((t + 1) % 2) * 1024:((t + 1) % 2) * 1024 + 1024],
                        x_h[128 * (t + 1):128 * (t + 2), :], f"x{t+1}")
            if en == "scalar":
                s = eng
                W(f"x{t}")
                if t >= 1:
                    W(f"sqfree{t-1}")
                I(sc_s, s.activation(sqb[:, :], xv, AF.Square), f"sq{t}")
            if en == "vector":
                v = eng
                W(f"x{t}")
                v.reduce_sum(smalls[:, 0:1], xv, axis=AX.X)
                W(f"sq{t}")
                rt(sc_v, v.reduce_sum(smalls[:, 1:2], sqb[:, :], axis=AX.X), f"sqfree{t}")
                rt(sc_v, v.tensor_scalar(smalls[:, 0:1], smalls[:, 0:1], 1.0 / D, None, ALU.mult))
                rt(sc_v, v.tensor_scalar(smalls[:, 1:2], smalls[:, 1:2], 1.0 / D, None, ALU.mult))
                rt(sc_v, v.tensor_tensor(smalls[:, 2:3], smalls[:, 0:1], smalls[:, 0:1], ALU.mult))
                rt(sc_v, v.tensor_tensor(smalls[:, 1:2], smalls[:, 1:2], smalls[:, 2:3], ALU.subtract))
                I(sc_v, v.tensor_scalar(smalls[:, 1:2], smalls[:, 1:2], 1e-5, None, ALU.add), f"var{t}")
            if en == "scalar":
                s = eng
                W(f"var{t}")
                rt(sc_s, s.activation(smalls[:, 3:4], smalls[:, 1:2], AF.Ln))
                I(sc_s, s.activation(smalls[:, 3:4], smalls[:, 3:4], AF.Exp, scale=-0.5), f"rstd{t}")
            if en == "vector":
                v = eng
                W(f"rstd{t}")
                if t >= 1:
                    W(f"xnTmm{t-1}")
                I(sc_v, v.tensor_scalar(xnb[:, :], xv, smalls[:, 0:1], smalls[:, 3:4],
                                        ALU.subtract, ALU.mult), f"xn{t}")
            for c in range(NC_):
                bank = "ps0b" if c % 2 == 0 else "ps1b"
                pd = ps0b if c % 2 == 0 else ps1b
                if en == "tensor":
                    te = eng
                    if c == 0:
                        W(f"xn{t}")
                        if t == 0:
                            W("tabs")
                    W(led[bank])
                    key = f"xnTm{t}_{c}"
                    I(sc_t, te.matmul(pd[:, 0:128], xnb[:, c * 128:c * 128 + 128], ident[:, :],
                                      is_transpose=True), key)
                    if c == NC_ - 1:
                        EV[f"xnTmm{t}"] = EV[key]
                if en == "scalar":
                    W(f"xnTm{t}_{c}")
                    I(sc_s, eng.copy(xnT[:, c * S + t * 128:c * S + t * 128 + 128], pd[:, 0:128]), f"xnTe{t}_{c}")
                led[bank] = f"xnTe{t}_{c}"

            if en == "tensor":
                te = eng
                for mi, (psX, wT) in enumerate(((ps2, wqT), (ps3, wkT), (ps4, wvT))):
                    W(f"xnTe{t}_{NC_-1}")
                    if mi == 0 and t == 0:
                        W("wtre2_63")
                    if t >= 1:
                        W([f"rope_q{t-1}", f"rope_k{t-1}", f"wrout{t-1}"][mi])
                    for c in range(NC_):
                        for hf in range(2):
                            last = te.matmul(psX[:, hf * 512:hf * 512 + 512],
                                             xnT[:, c * S + t * 128:c * S + t * 128 + 128],
                                             wT[:, c * 1024 + hf * 512:c * 1024 + hf * 512 + 512],
                                             start=(c == 0), stop=(c == NC_ - 1))
                    I(sc_t, last, f"mm{'qkv'[mi]}{t}")
            if en == "vector":
                v = eng
                for name, ps, rob, c0 in (("q", ps2, qro, 2048), ("k", ps3, kro, 0)):
                    W(f"mm{name}{t}")
                    if t >= 1:
                        W(f"{name}Tmm{t-1}")
                    r_in = ps.ap().rearrange("p (h f two) -> p h f two", h=H, two=2)
                    te_, to_ = r_in[:, :, :, 0:1], r_in[:, :, :, 1:2]
                    r_out = rob[:, :].rearrange("p (h f two) -> p h f two", h=H, two=2)
                    re_, ro_ = r_out[:, :, :, 0:1], r_out[:, :, :, 1:2]
                    cosb = tabs[:, c0 + 32 * t:c0 + 32 * t + 32].unsqueeze(1).broadcast_to([128, H, 32]).unsqueeze(3)
                    sinb = tabs[:, c0 + 1024 + 32 * t:c0 + 1056 + 32 * t].unsqueeze(1).broadcast_to([128, H, 32]).unsqueeze(3)
                    tA = tmA[:, :].rearrange("p (h f) -> p h f", h=H).unsqueeze(3)
                    tB = tmB[:, :].rearrange("p (h f) -> p h f", h=H).unsqueeze(3)
                    v.tensor_tensor(tA, te_, cosb, ALU.mult)
                    rt(sc_v, v.tensor_tensor(tB, to_, sinb, ALU.mult))
                    rt(sc_v, v.tensor_tensor(re_, tA, tB, ALU.subtract))
                    v.tensor_tensor(tA, te_, sinb, ALU.mult)
                    rt(sc_v, v.tensor_tensor(tB, to_, cosb, ALU.mult))
                    rt(sc_v, v.tensor_tensor(ro_, tA, tB, ALU.add), f"rope_{name}{t}")
            if en == "scalar":
                s = eng
                W(f"mmv{t}")
                if t >= 1:
                    W(f"vst{t-1}")
                for h in range(H):
                    last = s.copy(vsb[:, h * 65:h * 65 + 64], ps4[:, h * 64:h * 64 + 64])
                v_ev = I(sc_s, last, f"vev{t}")
                eng.wait_ge(sc_s.h, v_ev[1])
                dma(s, sems["sq_s"], v_s[t], vsb[:, :], f"vst{t}")
            if en == "tensor":
                te = eng
                W(f"vev{t}")
                if t == 0:
                    W("wr")
                for c in range(NC_):
                    last = te.matmul(ps4[:, 0:8], xnT[:, c * S + t * 128:c * S + t * 128 + 128],
                                     wrsb[:, c * E:c * E + E], start=(c == 0), stop=(c == NC_ - 1))
                I(sc_t, last, f"mmr{t}")
            if en == "vector":
                W(f"mmr{t}")
                I(sc_v, eng.reduce_max(smalls[:, 4:5], ps4[:, 0:8], axis=AX.X, negate=True), f"rmax{t}")
            if en == "scalar":
                W(f"rmax{t}")
                I(sc_s, eng.activation(rsm[:, :], ps4[:, 0:8], AF.Exp, bias=smalls[:, 4:5]), f"rexp{t}")
            if en == "vector":
                v = eng
                W(f"rexp{t}")
                rt(sc_v, v.reduce_sum(smalls[:, 5:6], rsm[:, :], axis=AX.X))
                rt(sc_v, v.reciprocal(smalls[:, 6:7], smalls[:, 5:6]))
                rt(sc_v, v.tensor_scalar(rsm[:, :], rsm[:, :], smalls[:, 6:7], None, ALU.mult))
                rt(sc_v, v.reduce_max(smalls[:, 7:8], rsm[:, :], axis=AX.X))
                rt(sc_v, v.tensor_scalar(rsm2[:, :], rsm[:, :], smalls[:, 7:8], None, ALU.is_equal))
                rt(sc_v, v.scalar_tensor_tensor(rsm3[:, :], rsm2[:, :], -1e30, rsm[:, :], ALU.mult, ALU.add))
                I(sc_v, v.reduce_max(smalls[:, 8:9], rsm3[:, :], axis=AX.X), f"rtop{t}")
            if en == "scalar":
                s = eng
                W(f"rtop{t}")
                s.activation(smalls[:, 9:10], smalls[:, 7:8], AF.Exp)
                I(sc_s, s.activation(smalls[:, 10:11], smalls[:, 8:9], AF.Exp), f"rexp2{t}")
            if en == "vector":
                v = eng
                W(f"rexp2{t}")
                rt(sc_v, v.tensor_tensor(smalls[:, 11:12], smalls[:, 9:10], smalls[:, 10:11], ALU.add))
                rt(sc_v, v.reciprocal(smalls[:, 12:13], smalls[:, 11:12]))
                rt(sc_v, v.tensor_scalar(rsm2[:, :], rsm2[:, :], smalls[:, 9:10], smalls[:, 12:13], ALU.mult, ALU.mult))
                rt(sc_v, v.tensor_scalar(rsm4[:, :], rsm3[:, :], smalls[:, 8:9], None, ALU.is_equal))
                rt(sc_v, v.tensor_scalar(rsm4[:, :], rsm4[:, :], smalls[:, 10:11], smalls[:, 12:13], ALU.mult, ALU.mult))
                I(sc_v, v.tensor_tensor(w_rout[:, t * E:t * E + E], rsm2[:, :], rsm4[:, :], ALU.add), f"wrout{t}")
            if en == "scalar":
                W(f"rope_q{t}")
                I(sc_s, eng.activation(sqb[:, :], qro[:, :], AF.Square), f"sqq{t}")
            if en == "vector":
                W(f"sqq{t}")
                I(sc_v, eng.reduce_sum(qn2t[:, :], sqb[:, :].rearrange("p (h f) -> p h f", h=H), axis=AX.X), f"qn2{t}")
            if en == "tensor":
                W(f"qn2{t}")
                W(led["ps1b"])
                I(sc_t, eng.transpose(ps1[0:16, 0:128], qn2t[:, :], identf[:, :]), f"qn2T{t}")
            if en == "vector":
                W(f"qn2T{t}")
                I(sc_v, eng.reduce_max(Mbuf[:, t:t + 1], ps1[0:16, 0:128], axis=AX.X), f"Mb{t}")
            led["ps1b"] = f"Mb{t}"
            if en == "scalar":
                W(f"rope_k{t}")
                W(f"qn2{t}")
                I(sc_s, eng.activation(sqb[:, :], kro[:, :], AF.Square), f"sqk{t}")
            if en == "vector":
                v = eng
                W(f"sqk{t}")
                rt(sc_v, v.reduce_sum(kn2t[:, :], sqb[:, :].rearrange("p (h f) -> p h f", h=H), axis=AX.X))
                I(sc_v, v.reduce_max(kmaxb[:, t:t + 1], kn2t[:, :], axis=AX.X), f"kn2{t}")
            for name, rob, tile, store in (("q", qro, qTt, qT_s), ("k", kro, kTt, kT_s)):
                for c in range(NC_):
                    bank = "ps0b" if c % 2 == 0 else "ps1b"
                    pd = ps0b if c % 2 == 0 else ps1b
                    if en == "tensor":
                        W(led[bank])
                        key = f"{name}Tm{t}_{c}"
                        I(sc_t, eng.matmul(pd[:, 0:128], rob[:, c * 128:c * 128 + 128], ident[:, :],
                                           is_transpose=True), key)
                        if c == NC_ - 1:
                            EV[f"{name}Tmm{t}"] = EV[key]
                    if en == "scalar":
                        W(f"{name}Tm{t}_{c}")
                        if c == 0 and t >= 1:
                            W(f"{name}st{t-1}")
                        I(sc_s, eng.copy(tile[:, c * 128:c * 128 + 128], pd[:, 0:128]), f"{name}Te{t}_{c}")
                    led[bank] = f"{name}Te{t}_{c}"
                if en == "scalar":
                    W(f"{name}Te{t}_{NC_-1}")
                    dma(eng, sems["sq_s"], store[t], tile[:, :], f"{name}st{t}")

        if en == "sync":
            for i_ in range(4):
                dma(eng, sems["sq_x"], out_h[i_ * 1024:(i_ + 1) * 1024, :],
                    x_h[i_ * 1024:(i_ + 1) * 1024, :], "xcopy" if i_ == 3 else None)

        if "p15" in FENCES:
            yield
        if en == "tensor":
            W(f"kn2{NT-1}")
            W(led["ps1b"])
            I(sc_t, eng.transpose(ps1[0:32, 0:128], kmaxb[:, :], identf[:, :]), "km1")
        if en == "vector":
            W("km1")
            I(sc_v, eng.reduce_max(kk1[:, :], ps1[0:32, 0:128], axis=AX.X), "km2")
        if en == "tensor":
            W("km2")
            I(sc_t, eng.transpose(ps1[0:1, 0:32], kk1[:, :], identf[0:32, 0:32]), "km3")
        if en == "vector":
            v = eng
            W("km3")
            I(sc_v, v.reduce_max(kk2[:, :], ps1[0:1, 0:32], axis=AX.X), "km4")
            W(f"Mb{NT-1}")
            I(sc_v, v.reduce_max(qn2blk[:, :], Mbuf[:, :].rearrange("p (b f) -> p b f", b=8), axis=AX.X), "qbm")
        if en == "tensor":
            W("km4")
            I(sc_t, eng.matmul(ps1[0:16, 0:1], ones_f[0:1, 0:16], kk2[0:1, :], start=True, stop=True), "km5")
        if en == "vector":
            v = eng
            W("km5")
            rt(sc_v, v.tensor_copy(maxk_col[:, :], ps1[0:16, 0:1]), "km6")
            W("qbm")
            I(sc_v, v.tensor_scalar(qn2blk[:, :], qn2blk[:, :], maxk_col[:, 0:1], None, ALU.mult), "prod")
        led["ps1b"] = "km6"
        if en == "scalar":
            s = eng
            W("prod")
            rt(sc_s, s.activation(negm16[:, :], qn2blk[:, :], AF.Ln))
            I(sc_s, s.activation(negm16[:, :], negm16[:, :], AF.Exp, scale=0.5), "negm_a")
        if en == "vector":
            W("negm_a")
            I(sc_v, eng.tensor_scalar(negm16[:, :], negm16[:, :], -1.01, None, ALU.mult), "negm")
        if en == "sync":
            sync = eng
            W("negm")
            v1 = dma(sync, sems["sq_x"], negm_d.rearrange("(h e) -> h e", h=16), negm16[:, :])
            sync.wait_ge(sems["sq_x"].h, v1[1])
            dma(sync, sems["sq_x"], negm_fl[0:1, :], negm_d.rearrange("(o f) -> o f", o=1), "negm_fl")
        if en == "tensor":
            W("negm_fl")
            W(led["ps1b"])
            I(sc_t, eng.matmul(ps1[:, 0:128], ones_f[0:1, :], negm_fl[0:1, :], start=True, stop=True), "biasT_mm")
        if en == "vector":
            W("biasT_mm")
            I(sc_v, eng.tensor_copy(biasT[:, :], ps1[:, 0:128]), "biasT")
        led["ps1b"] = "biasT"

        g_acc_ev = [None] * NT
        last_sil = [None, None]
        last_hm = [None, None]
        last_dn = None
        last_stg = {0: None, 1: None}
        last_gacc_stg = {0: None, 1: None}
        for e in range(E):
            if "exp" in FENCES:
                yield
            if en == "gpsimd":
                g = eng
                if e == 0:
                    W("biasT")
                    W(f"kst{NT-1}")
                else:
                    W(f"gu_done{e-1}")
                dma(g, sems["sq_w"], wgsb[:, :].rearrange("p (c f) -> p c f", c=NC_),
                    wg_h[e].rearrange("(c p) f -> p c f", p=128))
                dma(g, sems["sq_w"], wusb[:, :].rearrange("p (c f) -> p c f", c=NC_),
                    wu_h[e].rearrange("(c p) f -> p c f", p=128), f"wgu{e}")
                if e >= 1:
                    W(f"dn_done{e-1}")
                dma(g, sems["sq_w"], wdsb[:, :].rearrange("p (f d) -> p f d", f=NFC),
                    wd_h[e].rearrange("(f p) d -> p f d", p=128), f"wd{e}")
            for grp in range(NGRP):
                t0 = grp * GRP
                for fc in range(NFC):
                    psG = ps4 if fc % 2 == 0 else psA
                    if en == "tensor":
                        te = eng
                        if fc == 0 and grp == 0:
                            W(f"wgu{e}")
                            if e == 0:
                                W(f"xnTe{NT-1}_{NC_-1}")
                                W(f"wrout{NT-1}")
                        W(last_sil[0])
                        for c in range(NC_):
                            te.matmul(psG[:, 0:512], wgsb[:, c * FF + fc * 128:c * FF + fc * 128 + 128],
                                      xnT[:, c * S + t0:c * S + t0 + GRP], start=(c == 0), stop=(c == NC_ - 1))
                        W(last_hm[0])
                        for c in range(NC_):
                            last = te.matmul(psG[:, 512:1024], wusb[:, c * FF + fc * 128:c * FF + fc * 128 + 128],
                                             xnT[:, c * S + t0:c * S + t0 + GRP], start=(c == 0), stop=(c == NC_ - 1))
                        key = f"gu{e}_{grp}_{fc}"
                        I(sc_t, last, key)
                        if grp == NGRP - 1 and fc == NFC - 1:
                            EV[f"gu_done{e}"] = EV[key]
                    if en == "scalar":
                        W(f"gu{e}_{grp}_{fc}")
                        if fc == 0 and grp == 0 and e == 0:
                            W("biasT")
                        I(sc_s, eng.activation(hsil[:, (fc % 2) * GRP:(fc % 2) * GRP + GRP], psG[:, 0:512],
                                               AF.Silu), f"sil{e}_{grp}_{fc}")
                    last_sil = [last_sil[1], f"sil{e}_{grp}_{fc}"]
                    if en == "vector":
                        W(f"sil{e}_{grp}_{fc}")
                        W(last_dn)
                        I(sc_v, eng.tensor_tensor(hT[:, fc * GRP:fc * GRP + GRP], psG[:, 512:1024],
                                                  hsil[:, (fc % 2) * GRP:(fc % 2) * GRP + GRP],
                                                  ALU.mult), f"hm{e}_{grp}_{fc}")
                    last_hm = [last_hm[1], f"hm{e}_{grp}_{fc}"]
                for ti in range(4):
                    tt = grp * 4 + ti
                    par = ti % 2
                    psD = ps3 if par == 0 else ps2
                    if en == "tensor":
                        te = eng
                        if ti == 0:
                            W(f"hm{e}_{grp}_{NFC-1}")
                            if grp == 0:
                                W(f"wd{e}")
                                if e == 0:
                                    W(f"rope_q{NT-1}")
                                    W(f"rope_k{NT-1}")
                        W(last_stg[par])
                        for fc in range(NFC):
                            for hf in range(2):
                                last = te.matmul(psD[:, hf * 512:hf * 512 + 512],
                                                 hT[:, fc * GRP + ti * 128:fc * GRP + ti * 128 + 128],
                                                 wdsb[:, fc * 1024 + hf * 512:fc * 1024 + hf * 512 + 512],
                                                 start=(fc == 0), stop=(fc == NFC - 1))
                        key = f"dnm{e}_{tt}"
                        I(sc_t, last, key)
                        if ti == 3:
                            EV[f"dn{e}_{grp}"] = EV[key]
                            if grp == NGRP - 1:
                                EV[f"dn_done{e}"] = EV[key]
                    if ti == 3:
                        last_dn = f"dn{e}_{grp}"
                    if en == "vector":
                        W(f"dnm{e}_{tt}")
                        W(last_gacc_stg[tt % 2])
                        I(sc_v, eng.tensor_scalar(stg[:, (tt % 2) * 1024:(tt % 2) * 1024 + 1024], psD[:, :],
                                                  w_rout[:, tt * E + e:tt * E + e + 1], None, ALU.mult),
                          f"stg{e}_{tt}")
                    last_stg[par] = f"stg{e}_{tt}"
                    if en == "gpsimd":
                        g = eng
                        W(f"stg{e}_{tt}")
                        W(g_acc_ev[tt])
                        kw = dict(accum_op=ALU.add) if e >= 1 else {}
                        dma(g, sems["sq_w"], g_s[tt * 128:tt * 128 + 128, :],
                            stg[:, (tt % 2) * 1024:(tt % 2) * 1024 + 1024], f"gacc{e}_{tt}", **kw)
                    g_acc_ev[tt] = f"gacc{e}_{tt}"
                    last_gacc_stg[tt % 2] = f"gacc{e}_{tt}"

        P2_T = f"dn_done{E-1}"
        P2_V = f"stg{E-1}_{NT-1}"
        P2_G = f"gacc{E-1}_{NT-1}"

        if en == "gpsimd":
            g = eng
            W(P2_T); W(P2_V)
            dma(g, sems["sq_w"], stg_wo[:, :].rearrange("p (c d) -> p c d", c=8),
                wo_h.rearrange("(c p) d -> p c d", p=128), "wo_st")
        if en == "sync":
            sync = eng
            W(P2_T)
            for t in range(NT):
                dma(sync, sems["sq_x"], kTb[:, t * 1024:t * 1024 + 1024], kT_s[t],
                    "kT_ld" if t == NT - 1 else None)
        if en == "scalar":
            s = eng
            W(P2_T); W(P2_V); W(P2_G)
            for t in range(NT):
                dma(s, sems["sq_s"], vsb3[:, t * H * 65:(t + 1) * H * 65], v_s[t],
                    "v_ld" if t == NT - 1 else None)
        for k in range(64):
            ec, dc = k // 8, k % 8
            bank = "ps0b" if k % 2 == 0 else "ps1b"
            pd = ps0b if k % 2 == 0 else ps1b
            if en == "tensor":
                if k == 0:
                    W("wo_st")
                W(led[bank])
                I(sc_t, eng.matmul(pd[:, 0:128], stg_wo[:, ec * 1024 + dc * 128:ec * 1024 + dc * 128 + 128],
                                   ident[:, :], is_transpose=True), f"woTm{k}")
            if en == "scalar":
                W(f"woTm{k}")
                I(sc_s, eng.copy(woT[:, dc * 1024 + ec * 128:dc * 1024 + ec * 128 + 128], pd[:, 0:128]),
                  f"woTe{k}")
            led[bank] = f"woTe{k}"

        def emit_wo_tile(qbw, ti):
            tt = 4 * qbw + ti
            if en == "sync":
                if qbw == 0 and ti == 0:
                    W(P2_G)
                for pf in ([tt, tt + 1] if ti == 0 else ([tt + 1] if ti < 3 else [])):
                    if 4 * qbw <= pf < 4 * qbw + 4:
                        if pf >= 2:
                            W(f"ost{pf-2}")
                        dma(eng, sems["sq_x"], gb3[:, (pf % 2) * 1024:(pf % 2) * 1024 + 1024],
                            g_s[pf * 128:pf * 128 + 128, :], f"g_ld{pf}")
            if en == "scalar":
                W(f"g_ld{tt}")
                if tt >= 1:
                    W(f"comb{tt-1}")
                I(sc_s, eng.activation(gb3s[:, :], gb3[:, (tt % 2) * 1024:(tt % 2) * 1024 + 1024],
                                       AF.Silu), f"gsil{tt}")
            if en == "tensor":
                te = eng
                W(f"ont{qbw}_{H-1}")
                if tt == 0:
                    W("woTe63")
                if tt >= 1:
                    W(f"comb{tt-1}")
                for dc in range(NC_):
                    for hf in range(2):
                        last = te.matmul(ps4[:, hf * 512:hf * 512 + 512],
                                         ontT[:, (qbw % 2) * 4096 + dc * 512 + ti * 128:(qbw % 2) * 4096 + dc * 512 + ti * 128 + 128],
                                         woT[:, dc * 1024 + hf * 512:dc * 1024 + hf * 512 + 512],
                                         start=(dc == 0), stop=(dc == NC_ - 1))
                I(sc_t, last, f"womm{tt}")
            if en == "vector":
                v = eng
                W(f"womm{tt}")
                W(f"gsil{tt}")
                if tt >= 1:
                    W(f"ost{tt-1}")
                I(sc_v, v.tensor_tensor(gb3[:, (tt % 2) * 1024:(tt % 2) * 1024 + 1024],
                                        ps4[:, :], gb3s[:, :], ALU.mult), f"comb{tt}")
            if en == "gpsimd":
                g = eng
                W(f"comb{tt}")
                if tt == 0:
                    W("xcopy")
                dma(g, sems["sq_w"], out_h[tt * 128:tt * 128 + 128, :],
                    gb3[:, (tt % 2) * 1024:(tt % 2) * 1024 + 1024], f"ost{tt}", accum_op=ALU.add)

        pair_i = 0
        for qb in range(NQB):
            if "qb" in FENCES:
                yield
            if en == "sync":
                sync = eng
                if qb == 0:
                    W(P2_T)
                else:
                    W(f"sc_last{qb-1}")
                dma(sync, sems["sq_x"], qblk[:, :].rearrange("p (t c) -> p t c", t=4),
                    qT_s[4 * qb:4 * qb + 4].transpose([1, 0, 2]), f"q_ld{qb}")
            npair = 2 * qb + 2
            for h in range(H):
                if qb >= 1 and 1 <= h <= 4:
                    emit_wo_tile(qb - 1, h - 1)
                hp, hr = h // 2, h % 2
                cps0 = pair_i
                for idx in range(npair + 1):
                    if idx < npair:
                        pp = idx
                        cp = cps0 + pp
                        c0, c1_ = 2 * pp, 2 * pp + 1
                        psS = psA.ap() if cp % 2 == 0 else ps2.ap()
                        pth = Pt[:, (cp % 2) * 1024:(cp % 2) * 1024 + 1024]
                        if en == "tensor":
                            te = eng
                            if h == 0 and pp == 0:
                                W("kT_ld")
                                W(f"q_ld{qb}")
                                if qb == 0:
                                    W("biasT")
                                    W(P2_V)
                            if cp >= 2:
                                W(f"expP{cp-2}")
                            rq = qblk[hr * 64:hr * 64 + 64, :].rearrange("p (t c2) -> p t c2", t=4)[:, :, hp * 128:hp * 128 + 128]
                            te.matmul(psS[:, 0:512], kTb[hr * 64:hr * 64 + 64, c0 * 1024 + hp * 128:c0 * 1024 + hp * 128 + 128],
                                      rq, start=True, stop=True)
                            I(sc_t, te.matmul(psS[:, 512:1024], kTb[hr * 64:hr * 64 + 64, c1_ * 1024 + hp * 128:c1_ * 1024 + hp * 128 + 128],
                                              rq, start=True, stop=True), f"smmP{cp}")
                        if en == "scalar":
                            W(f"smmP{cp}")
                            if cp >= 2:
                                W(f"avP{cp-2}")
                            I(sc_s, eng.activation(pth, psS[:, :], AF.Exp,
                                                   bias=biasT[:, h * 8 + qb:h * 8 + qb + 1]), f"expP{cp}")
                        if c0 >= 4 * qb and en == "vector":
                            W(f"expP{cp}")
                            tloc = c0 - 4 * qb
                            I(sc_v, eng.tensor_tensor(pth, pth, masks[:, tloc * 512:tloc * 512 + 1024], ALU.mult),
                              f"mskP{cp}")
                    if idx >= 1:
                        pp = idx - 1
                        cp = cps0 + pp
                        c0, c1_ = 2 * pp, 2 * pp + 1
                        boundary = c0 >= 4 * qb
                        pth = Pt[:, (cp % 2) * 1024:(cp % 2) * 1024 + 1024]
                        if en == "tensor":
                            te = eng
                            W(f"mskP{cp}" if boundary else f"expP{cp}")
                            if pp == 0:
                                if h >= 2:
                                    W(f"ont{qb}_{h-2}")
                                elif qb >= 1:
                                    W(f"ont{qb-1}_{H-2+h}")
                                if h == 0 and qb == 0:
                                    W("v_ld")
                            psV = ps3.ap()[0:65, (h % 2) * 512:(h % 2) * 512 + 512]
                            te.matmul(psV, vsb3[:, c0 * H * 65 + h * 65:c0 * H * 65 + h * 65 + 65],
                                      pth[:, 0:512], start=(c0 == 0), stop=False)
                            key = f"avP{cp}"
                            I(sc_t, te.matmul(psV, vsb3[:, c1_ * H * 65 + h * 65:c1_ * H * 65 + h * 65 + 65],
                                              pth[:, 512:1024], start=False, stop=(c1_ == 4 * qb + 3)), key)
                            if c1_ == 4 * qb + 3:
                                EV[f"av{qb}_{h}"] = EV[key]
                                if h == H - 1:
                                    EV[f"sc_last{qb}"] = EV[key]
                pair_i += npair
                if en == "scalar":
                    s = eng
                    W(f"av{qb}_{h}")
                    psF = ps3.ap()
                    rt(sc_s, s.activation(lnrow[0:1, :], psF[64:65, (h % 2) * 512:(h % 2) * 512 + 512], AF.Ln))
                    I(sc_s, s.activation(rdrow[0:1, :], lnrow[0:1, :], AF.Exp, scale=-1.0), f"rd{qb}_{h}")
                if en == "tensor":
                    W(f"rd{qb}_{h}")
                    if qb == 0 and h < 2:
                        W(led["ps1b" if h % 2 else "ps0b"])
                    I(sc_t, eng.matmul(ps3.ap()[64:128, (h % 2) * 512:(h % 2) * 512 + 512],
                                       ones_f[0:1, 0:64], rdrow[0:1, :], start=True, stop=True),
                      f"rdbcm{qb}_{h}")
                if en == "vector":
                    v = eng
                    W(f"rdbcm{qb}_{h}")
                    if h == 0 and qb >= 2:
                        W(f"womm{4*(qb-1)-1}")
                    rt(sc_v, v.tensor_copy(rdbc[:, :], ps3.ap()[64:128, (h % 2) * 512:(h % 2) * 512 + 512]),
                       f"rdbce{qb}_{h}")
                    I(sc_v, v.tensor_tensor(ontT[hr * 64:hr * 64 + 64, (qb % 2) * 4096 + hp * 512:(qb % 2) * 4096 + hp * 512 + 512],
                                            ps3[0:64, (h % 2) * 512:(h % 2) * 512 + 512],
                                            rdbc[0:64, :], ALU.mult), f"ont{qb}_{h}")
            if qb == NQB - 1:
                for ti in range(4):
                    emit_wo_tile(qb, ti)

        if en == "scalar":
            EV["tot_sq_s"] = ("sq_s", sems["sq_s"].n)
        if en == "gpsimd":
            EV["tot_sq_w"] = ("sq_w", sems["sq_w"].n)
        if en == "sync":
            eng.wait_ge(sems["sq_x"].h, sems["sq_x"].n)
            W("tot_sq_s")
            W("tot_sq_w")

    n_fence = 1 + (1 if "p0" in FENCES else 0) + (NT if "tile" in FENCES else 0) \
        + (1 if "p15" in FENCES else 0) + (E if "exp" in FENCES else 0) + (NQB if "qb" in FENCES else 0)
    ENG = ("sync", "gpsimd", "scalar", "vector", "tensor")
    holders = {en: [None] for en in ENG}
    gens = {en: walk(en, holders[en]) for en in ENG}
    for en in ENG:
        next(gens[en])
    with nc.Block() as blk:
        for seg in range(n_fence):
            for en in ENG:
                def run(engobj, en=en):
                    holders[en][0] = engobj
                    try:
                        next(gens[en])
                    except StopIteration:
                        pass
                getattr(blk, en)(run)
    return nc, EV


def build(dbg=()):
    global EVp
    EVp = {}
    _, ev1 = build_once(dbg, EVp, final=False)
    EVp = ev1
    nc, ev2 = build_once(dbg, EVp, final=True)
    assert ev1 == ev2, "event mismatch between passes"
    return nc


EVp = {}
_NC_CACHE = {}


def kernel(**inputs):
    if "nc" not in _NC_CACHE:
        _NC_CACHE["nc"] = build()
    nc = _NC_CACHE["nc"]
    x = np.ascontiguousarray(inputs["L0_attn_x"])
    base = {
        "wq": np.ascontiguousarray(inputs["L0_attn_Wq"]),
        "wk": np.ascontiguousarray(inputs["L0_attn_Wk"]),
        "wv": np.ascontiguousarray(inputs["L0_attn_Wv"]),
        "wo": np.ascontiguousarray(inputs["L0_attn_Wo"]),
        "wr": np.ascontiguousarray(inputs["L0_ffn_kernel_router_DE"]),
        "wg": np.ascontiguousarray(inputs["L0_ffn_kernel_gating_EDF"]),
        "wu": np.ascontiguousarray(inputs["L0_ffn_kernel_up_EDF"]),
        "wd": np.ascontiguousarray(inputs["L0_ffn_kernel_down_EFD"]),
        "invf": np.ascontiguousarray(np.broadcast_to(
            (10000.0 ** (-np.arange(0, 64, 2, dtype=np.float64) / 64.0)).astype(np.float32)[None, :],
            (128, 32))),
    }
    in_maps = [dict(base, x=np.ascontiguousarray(x[b])) for b in range(8)]
    res = run_bass_kernel_spmd(nc, in_maps, core_ids=list(range(8)))
    return np.stack([res.results[b]["out"] for b in range(8)], axis=0)


# version 10
# speedup vs baseline: 1.3597x; 1.0284x over previous
import math
import sys
import numpy as np
try:
    import concourse.bass as bass
except ImportError:
    sys.path.insert(0, "/opt/trn_rl_repo")
import concourse.bass as bass
import concourse.mybir as mybir
from concourse.bass_utils import run_bass_kernel_spmd

F32 = mybir.dt.float32
BF16 = mybir.dt.bfloat16
I32 = mybir.dt.int32
AF = mybir.ActivationFunctionType
ALU = mybir.AluOpType
AX = mybir.AxisListType

S, D, H, HD, E, FF = 4096, 1024, 16, 64, 8, 2048
NT = S // 128
NC_ = D // 128
NFC = FF // 128
NQB = S // 512
GRP = 512
NGRP = S // GRP

PI = math.pi
HI2PI = 6.28125
LO2PI = 2 * PI - HI2PI
INV2PI = 1.0 / (2 * PI)


class Sem:
    def __init__(self, nc, name):
        self.h = nc.alloc_semaphore(name)
        self.name = name
        self.n = 0


FENCES = ("p0", "tile", "p15", "exp", "qb")


def build_once(dbg, EVp, final):
    nc = bass.Bass("TRN2", target_bir_lowering=False)
    dbg = set(dbg)

    x_h = nc.dram_tensor("x", [S, D], F32, kind="ExternalInput").ap()
    wq_h = nc.dram_tensor("wq", [D, D], F32, kind="ExternalInput").ap()
    wk_h = nc.dram_tensor("wk", [D, D], F32, kind="ExternalInput").ap()
    wv_h = nc.dram_tensor("wv", [D, D], F32, kind="ExternalInput").ap()
    wo_h = nc.dram_tensor("wo", [D, D], F32, kind="ExternalInput").ap()
    wr_h = nc.dram_tensor("wr", [D, E], F32, kind="ExternalInput").ap()
    wg_h = nc.dram_tensor("wg", [E, D, FF], F32, kind="ExternalInput").ap()
    wu_h = nc.dram_tensor("wu", [E, D, FF], F32, kind="ExternalInput").ap()
    wd_h = nc.dram_tensor("wd", [E, FF, D], F32, kind="ExternalInput").ap()
    invf_h = nc.dram_tensor("invf", [128, 32], F32, kind="ExternalInput").ap()
    out_h = nc.dram_tensor("out", [S, D], F32, kind="ExternalOutput").ap()

    def scratch(name, shape, dtype):
        kind = "ExternalOutput" if name in dbg else "Internal"
        return nc.dram_tensor(name, shape, dtype, kind=kind).ap()

    qT_s = scratch("qT_s", [NT, 128, 1024], BF16)
    kT_s = scratch("kT_s", [NT, 128, 1024], BF16)
    v_s = scratch("v_s", [NT, 128, H * 65], BF16)
    g_s = scratch("g_s", [S, D], F32)
    wg_b = scratch("wg_b", [E, D, FF], BF16)
    wu_b = scratch("wu_b", [E, D, FF], BF16)
    wd_b = scratch("wd_b", [E, FF, D], BF16)
    negm_d = scratch("negm_d", [128], F32)

    cur = [20480]
    off = {}

    def carve(name, shape, dtype, at=None):
        size = {F32: 4, BF16: 2, I32: 4}[dtype]
        n_free = int(np.prod(shape[1:]))
        if at is None:
            at = (cur[0] + 31) & ~31
            cur[0] = at + n_free * size
            assert cur[0] <= 229370, (name, cur[0])
        off[name] = at
        return nc.alloc_sbuf_tensor_at(name, list(shape), dtype, offset=at)

    ident = carve("ident", [128, 128], BF16)
    identf = carve("identf", [128, 128], F32)
    masks = carve("masks", [128, 4 * 512], BF16)
    biasT = carve("biasT", [128, 128], F32)
    w_rout = carve("w_rout", [128, NT * E], F32)
    Mbuf = carve("Mbuf", [16, NT], F32)
    kmaxb = carve("kmaxb", [128, NT], F32)
    invfb = carve("invfb", [128, 32], F32)
    ones_f = carve("ones_f", [1, 128], F32)
    posf = carve("posf", [128, NT], F32)
    posi = carve("posi", [128, NT], I32)
    maxk_col = carve("maxk_col", [16, 1], F32)
    negm16 = carve("negm16", [16, 8], F32)
    negm_fl = carve("negm_fl", [1, 128], F32)
    qn2blk = carve("qn2blk", [16, 8], F32)
    smalls = carve("smalls", [128, 16], F32)
    qn2t = carve("qn2t", [128, 16], F32)
    kn2t = carve("kn2t", [128, 16], F32)
    rsm = carve("rsm", [128, 8], F32)
    rsm2 = carve("rsm2", [128, 8], F32)
    rsm3 = carve("rsm3", [128, 8], F32)
    rsm4 = carve("rsm4", [128, 8], F32)
    kk1 = carve("kk1", [32, 1], F32)
    kk2 = carve("kk2", [1, 1], F32)
    angb = carve("angb", [128, 32], F32)
    ubb = carve("ubb", [128, 32], F32)
    nbb = carve("nbb", [128, 32], F32)
    tb2 = carve("tb2", [128, 32], F32)
    PERM_END = (cur[0] + 31) & ~31

    R1 = PERM_END
    xnT = carve("xnT", [128, NC_ * S], BF16, at=R1)
    kTb = carve("kTb", [128, NT * 1024], BF16, at=R1)
    R2 = R1 + 65536

    c1 = [R2]

    def carve1(name, shape, dtype):
        size = {F32: 4, BF16: 2, I32: 4}[dtype]
        at = (c1[0] + 31) & ~31
        c1[0] = at + int(np.prod(shape[1:])) * size
        assert c1[0] <= 229370, (name, c1[0])
        return carve(name, shape, dtype, at=at)

    tabs = carve1("tabs", [128, 4 * 1024], F32)
    angF = carve1("angF", [128, 1024], F32)
    tbF = carve1("tbF", [128, 1024], F32)
    ubF = carve1("ubF", [128, 1024], F32)
    nbF = carve1("nbF", [128, 1024], F32)
    wqT = carve1("wqT", [128, NC_ * 1024], BF16)
    wkT = carve1("wkT", [128, NC_ * 1024], BF16)
    wvT = carve1("wvT", [128, NC_ * 1024], BF16)
    xb = carve1("xb", [128, 2 * 1024], F32)
    sqb = carve1("sqb", [128, 1024], F32)
    xnb = carve1("xnb", [128, 1024], BF16)
    qro = carve1("qro", [128, 1024], BF16)
    kro = carve1("kro", [128, 1024], BF16)
    tmA = carve1("tmA", [128, 512], F32)
    tmB = carve1("tmB", [128, 512], F32)
    vsb = carve1("vsb", [128, H * 65], BF16)
    qTt = carve1("qTt", [128, 1024], BF16)
    kTt = carve1("kTt", [128, 1024], BF16)
    wrsb = carve1("wrsb", [128, NC_ * E], BF16)
    stg_w = carve("stg_w", [128, 8192], BF16, at=off["xb"])

    c2 = [R2]

    def carve2(name, shape, dtype):
        size = {F32: 4, BF16: 2}[dtype]
        at = (c2[0] + 31) & ~31
        c2[0] = at + int(np.prod(shape[1:])) * size
        assert c2[0] <= 229370, (name, c2[0])
        return carve(name, shape, dtype, at=at)

    wgsb = carve2("wgsb", [128, NC_ * FF], BF16)
    wusb = carve2("wusb", [128, NC_ * FF], BF16)
    wdsb = carve2("wdsb", [128, NFC * 1024], BF16)
    hT = carve2("hT", [128, NFC * GRP], BF16)
    hsil = carve2("hsil", [128, 2 * GRP], F32)
    stg = carve2("stg", [128, 2 * 1024], F32)

    c3 = [R2]

    def carve3(name, shape, dtype):
        size = {F32: 4, BF16: 2}[dtype]
        at = (c3[0] + 31) & ~31
        c3[0] = at + int(np.prod(shape[1:])) * size
        assert c3[0] <= 229370, (name, c3[0])
        return carve(name, shape, dtype, at=at)

    vsb3 = carve3("vsb3", [128, NT * H * 65], BF16)
    qblk = carve3("qblk", [128, 4 * 1024], BF16)
    woT = carve3("woT", [128, NC_ * 1024], BF16)
    Pt = carve3("Pt", [128, 2 * 1024], BF16)
    ontT = carve3("ontT", [128, 2 * NC_ * 512], BF16)
    rdbc = carve3("rdbc", [64, 512], F32)
    lnrow = carve3("lnrow", [1, 512], F32)
    rdrow = carve3("rdrow", [1, 512], F32)
    gb3 = carve3("gb3", [128, 2 * 1024], F32)
    gb3s = carve3("gb3s", [128, 1024], F32)
    stg_wo = carve("stg_wo", [128, 8192], BF16, at=off["gb3"])

    psA = nc.alloc_psum_tensor("psA", [128, 1024], F32)
    ps2 = nc.alloc_psum_tensor("ps2", [128, 1024], F32)
    ps3 = nc.alloc_psum_tensor("ps3", [128, 1024], F32)
    ps4 = nc.alloc_psum_tensor("ps4", [128, 1024], F32)
    psAb = psA.ap().bitcast(BF16)
    ps0b = psAb[:, 0:1024]
    ps1b = psAb[:, 1024:2048]
    ps1 = psA.ap()[:, 512:1024]

    sems = {n: Sem(nc, n) for n in ("sq_x", "sq_s", "sq_w", "sc_v", "sc_s", "sc_t", "sc_g")}
    EV = {}

    def walk(en, holder):
        class _P:
            def __getattr__(self, a):
                return getattr(holder[0], a)
        eng = _P()
        yield
        sq_x, sq_s, sq_w = sems["sq_x"], sems["sq_s"], sems["sq_w"]
        sc_v, sc_s, sc_t, sc_g = sems["sc_v"], sems["sc_s"], sems["sc_t"], sems["sc_g"]

        def ev(key):
            v = EV.get(key)
            if v is None:
                v = EVp.get(key)
            if final:
                assert v is not None, key
            return v

        def W(key):
            v = ev(key) if isinstance(key, str) else key
            if v is not None and v[1] > 0:
                eng.wait_ge(sems[v[0]].h, v[1])

        def I(sem, instr, key=None, k=1):
            instr.then_inc(sem.h, k)
            sem.n += k
            if key is not None:
                EV[key] = (sem.name, sem.n)
            return (sem.name, sem.n)

        def rt(sem, instr, key=None):
            v = I(sem, instr, key)
            eng.wait_ge(sem.h, v[1])
            return v

        def dma(q, sem, out, in_, key=None, **kw):
            q.dma_start(out, in_, **kw).then_inc(sem.h, 16)
            sem.n += 16
            if key is not None:
                EV[key] = (sem.name, sem.n)
            return (sem.name, sem.n)

        led = {"ps0b": None, "ps1b": None}

        if en == "gpsimd":
            g = eng
            rt(sc_g, g.memset(ident[:, :], 1.0))
            g.affine_select(ident[:, :], ident[:, :], [[-1, 128]], ALU.is_equal, 0.0, base=0, channel_multiplier=1)
            rt(sc_g, g.memset(identf[:, :], 1.0))
            g.affine_select(identf[:, :], identf[:, :], [[-1, 128]], ALU.is_equal, 0.0, base=0, channel_multiplier=1)
            rt(sc_g, g.memset(masks[:, :], 1.0))
            for t in range(4):
                rt(sc_g, g.affine_select(masks[:, 512 * t:512 * (t + 1)], masks[:, 512 * t:512 * (t + 1)],
                                         [[1, 512]], ALU.is_ge, 0.0, base=-128 * t, channel_multiplier=-1))
            g.memset(vsb[:, :].rearrange("p (h c) -> p h c", h=H)[:, :, 64:65], 1.0)
            rt(sc_g, g.iota(posi[:, :], [[128, NT]], base=0, channel_multiplier=1))
            I(sc_g, g.tensor_copy(posf[:, :], posi[:, :]), "p0g")
            dma(g, sq_w, wrsb[:, :].rearrange("p (c e) -> p c e", c=NC_),
                wr_h.rearrange("(c p) e -> p c e", p=128), "wr")
        if en == "sync":
            dma(eng, sems["sq_x"], invfb[:, :], invf_h, "invfb")
        if en == "vector":
            I(sc_v, eng.memset(ones_f[:, :], 1.0), "ones")

        if en == "vector":
            v = eng
            W("invfb"); W("p0g")
            for t in range(NT):
                last = v.tensor_scalar(angF[:, 32 * t:32 * t + 32], invfb[:, :], posf[:, t:t + 1], None, ALU.mult)
            rt(sc_v, last)
            rt(sc_v, v.tensor_scalar(ubF[:, :], angF[:, :], INV2PI, 8388608.0, ALU.mult, ALU.add))
            rt(sc_v, v.tensor_scalar(nbF[:, :], ubF[:, :], -8388608.0, None, ALU.add))
            rt(sc_v, v.scalar_tensor_tensor(tbF[:, :], nbF[:, :], -HI2PI, angF[:, :], ALU.mult, ALU.add))
            rt(sc_v, v.scalar_tensor_tensor(tbF[:, :], nbF[:, :], -LO2PI, tbF[:, :], ALU.mult, ALU.add))
            rt(sc_v, v.tensor_scalar(ubF[:, :], angF[:, :], INV2PI, 0.25, ALU.mult, ALU.add))
            rt(sc_v, v.tensor_scalar(ubF[:, :], ubF[:, :], 8388608.0, None, ALU.add))
            rt(sc_v, v.tensor_scalar(nbF[:, :], ubF[:, :], -8388608.0, None, ALU.add))
            rt(sc_v, v.scalar_tensor_tensor(ubF[:, :], nbF[:, :], -HI2PI, angF[:, :], ALU.mult, ALU.add))
            rt(sc_v, v.scalar_tensor_tensor(ubF[:, :], nbF[:, :], -LO2PI, ubF[:, :], ALU.mult, ALU.add))
            I(sc_v, v.tensor_scalar(ubF[:, :], ubF[:, :], PI / 2, None, ALU.add), "angs")
        if en == "scalar":
            s = eng
            W("angs")
            s.activation(tabs[:, 1024:2048], tbF[:, :], AF.Sin)
            I(sc_s, s.activation(tabs[:, 0:1024], ubF[:, :], AF.Sin), "sins")
        if en == "vector":
            v = eng
            W("sins")
            v.tensor_scalar(tabs[:, 2048:3072], tabs[:, 0:1024], 0.125, None, ALU.mult)
            I(sc_v, v.tensor_scalar(tabs[:, 3072:4096], tabs[:, 1024:2048], 0.125, None, ALU.mult), "tabs")

        for i, (wh, wT) in enumerate(((wq_h, wqT), (wk_h, wkT), (wv_h, wvT))):
            if en == "gpsimd":
                if i > 0:
                    W(f"wtre{i-1}_63")
                dma(eng, sems["sq_w"], stg_w[:, :].rearrange("p (c d) -> p c d", c=8),
                    wh.rearrange("(c p) d -> p c d", p=128), f"wst{i}")
            for k in range(64):
                ec, dc = k // 8, k % 8
                bank = "ps0b" if k % 2 == 0 else "ps1b"
                pd = ps0b if k % 2 == 0 else ps1b
                if en == "tensor":
                    if k == 0:
                        W(f"wst{i}")
                        W("p0g")
                    W(led[bank])
                    I(sc_t, eng.matmul(pd[:, 0:128], stg_w[:, ec * 1024 + dc * 128:ec * 1024 + dc * 128 + 128],
                                       ident[:, :], is_transpose=True), f"wtrm{i}_{k}")
                if en == "scalar":
                    W(f"wtrm{i}_{k}")
                    I(sc_s, eng.copy(wT[:, dc * 1024 + ec * 128:dc * 1024 + ec * 128 + 128], pd[:, 0:128]),
                      f"wtre{i}_{k}")
                led[bank] = f"wtre{i}_{k}"

        if en == "sync":
            W("wtre2_63")
            dma(eng, sems["sq_x"], xb[:, 0:1024], x_h[0:128, :], "x0")
        if en == "gpsimd":
            for e_ in range(E):
                dma(eng, sems["sq_w"], wg_b[e_], wg_h[e_])
                dma(eng, sems["sq_w"], wu_b[e_], wu_h[e_], f"wgc{e_}")
                dma(eng, sems["sq_w"], wd_b[e_], wd_h[e_], f"wdc{e_}")
        if "p0" in FENCES:
            yield

        for t in range(NT):
            if "tile" in FENCES:
                yield
            xv = xb[:, (t % 2) * 1024:(t % 2) * 1024 + 1024]
            if en == "sync":
                if t + 1 < NT:
                    if t >= 1:
                        W(f"xn{t-1}"); W(f"sq{t-1}")
                    dma(eng, sems["sq_x"], xb[:, ((t + 1) % 2) * 1024:((t + 1) % 2) * 1024 + 1024],
                        x_h[128 * (t + 1):128 * (t + 2), :], f"x{t+1}")
            if en == "scalar":
                s = eng
                W(f"x{t}")
                if t >= 1:
                    W(f"sqfree{t-1}")
                I(sc_s, s.activation(sqb[:, :], xv, AF.Square), f"sq{t}")
            if en == "vector":
                v = eng
                W(f"x{t}")
                v.reduce_sum(smalls[:, 0:1], xv, axis=AX.X)
                W(f"sq{t}")
                rt(sc_v, v.reduce_sum(smalls[:, 1:2], sqb[:, :], axis=AX.X), f"sqfree{t}")
                rt(sc_v, v.tensor_scalar(smalls[:, 0:1], smalls[:, 0:1], 1.0 / D, None, ALU.mult))
                rt(sc_v, v.tensor_scalar(smalls[:, 1:2], smalls[:, 1:2], 1.0 / D, None, ALU.mult))
                rt(sc_v, v.tensor_tensor(smalls[:, 2:3], smalls[:, 0:1], smalls[:, 0:1], ALU.mult))
                rt(sc_v, v.tensor_tensor(smalls[:, 1:2], smalls[:, 1:2], smalls[:, 2:3], ALU.subtract))
                I(sc_v, v.tensor_scalar(smalls[:, 1:2], smalls[:, 1:2], 1e-5, None, ALU.add), f"var{t}")
            if en == "scalar":
                s = eng
                W(f"var{t}")
                rt(sc_s, s.activation(smalls[:, 3:4], smalls[:, 1:2], AF.Ln))
                I(sc_s, s.activation(smalls[:, 3:4], smalls[:, 3:4], AF.Exp, scale=-0.5), f"rstd{t}")
            if en == "vector":
                v = eng
                W(f"rstd{t}")
                if t >= 1:
                    W(f"xnTmm{t-1}")
                I(sc_v, v.tensor_scalar(xnb[:, :], xv, smalls[:, 0:1], smalls[:, 3:4],
                                        ALU.subtract, ALU.mult), f"xn{t}")
            for c in range(NC_):
                bank = "ps0b" if c % 2 == 0 else "ps1b"
                pd = ps0b if c % 2 == 0 else ps1b
                if en == "tensor":
                    te = eng
                    if c == 0:
                        W(f"xn{t}")
                        if t == 0:
                            W("tabs")
                    W(led[bank])
                    key = f"xnTm{t}_{c}"
                    I(sc_t, te.matmul(pd[:, 0:128], xnb[:, c * 128:c * 128 + 128], ident[:, :],
                                      is_transpose=True), key)
                    if c == NC_ - 1:
                        EV[f"xnTmm{t}"] = EV[key]
                if en == "scalar":
                    W(f"xnTm{t}_{c}")
                    I(sc_s, eng.copy(xnT[:, c * S + t * 128:c * S + t * 128 + 128], pd[:, 0:128]), f"xnTe{t}_{c}")
                led[bank] = f"xnTe{t}_{c}"

            if en == "tensor":
                te = eng
                for mi, (psX, wT) in enumerate(((ps2, wqT), (ps3, wkT), (ps4, wvT))):
                    W(f"xnTe{t}_{NC_-1}")
                    if mi == 0 and t == 0:
                        W("wtre2_63")
                    if t >= 1:
                        W([f"rope_q{t-1}", f"rope_k{t-1}", f"wrout{t-1}"][mi])
                    for c in range(NC_):
                        for hf in range(2):
                            last = te.matmul(psX[:, hf * 512:hf * 512 + 512],
                                             xnT[:, c * S + t * 128:c * S + t * 128 + 128],
                                             wT[:, c * 1024 + hf * 512:c * 1024 + hf * 512 + 512],
                                             start=(c == 0), stop=(c == NC_ - 1))
                    I(sc_t, last, f"mm{'qkv'[mi]}{t}")
            if en == "vector":
                v = eng
                for name, ps, rob, c0 in (("q", ps2, qro, 2048), ("k", ps3, kro, 0)):
                    W(f"mm{name}{t}")
                    if t >= 1:
                        W(f"{name}Tmm{t-1}")
                    r_in = ps.ap().rearrange("p (h f two) -> p h f two", h=H, two=2)
                    te_, to_ = r_in[:, :, :, 0:1], r_in[:, :, :, 1:2]
                    r_out = rob[:, :].rearrange("p (h f two) -> p h f two", h=H, two=2)
                    re_, ro_ = r_out[:, :, :, 0:1], r_out[:, :, :, 1:2]
                    cosb = tabs[:, c0 + 32 * t:c0 + 32 * t + 32].unsqueeze(1).broadcast_to([128, H, 32]).unsqueeze(3)
                    sinb = tabs[:, c0 + 1024 + 32 * t:c0 + 1056 + 32 * t].unsqueeze(1).broadcast_to([128, H, 32]).unsqueeze(3)
                    tA = tmA[:, :].rearrange("p (h f) -> p h f", h=H).unsqueeze(3)
                    tB = tmB[:, :].rearrange("p (h f) -> p h f", h=H).unsqueeze(3)
                    v.tensor_tensor(tA, te_, cosb, ALU.mult)
                    rt(sc_v, v.tensor_tensor(tB, to_, sinb, ALU.mult))
                    rt(sc_v, v.tensor_tensor(re_, tA, tB, ALU.subtract))
                    v.tensor_tensor(tA, te_, sinb, ALU.mult)
                    rt(sc_v, v.tensor_tensor(tB, to_, cosb, ALU.mult))
                    rt(sc_v, v.tensor_tensor(ro_, tA, tB, ALU.add), f"rope_{name}{t}")
            if en == "scalar":
                s = eng
                W(f"mmv{t}")
                if t >= 1:
                    W(f"vst{t-1}")
                for h in range(H):
                    last = s.copy(vsb[:, h * 65:h * 65 + 64], ps4[:, h * 64:h * 64 + 64])
                v_ev = I(sc_s, last, f"vev{t}")
                eng.wait_ge(sc_s.h, v_ev[1])
                dma(s, sems["sq_s"], v_s[t], vsb[:, :], f"vst{t}")
            if en == "tensor":
                te = eng
                W(f"vev{t}")
                if t == 0:
                    W("wr")
                for c in range(NC_):
                    last = te.matmul(ps4[:, 0:8], xnT[:, c * S + t * 128:c * S + t * 128 + 128],
                                     wrsb[:, c * E:c * E + E], start=(c == 0), stop=(c == NC_ - 1))
                I(sc_t, last, f"mmr{t}")
            if en == "vector":
                W(f"mmr{t}")
                I(sc_v, eng.reduce_max(smalls[:, 4:5], ps4[:, 0:8], axis=AX.X, negate=True), f"rmax{t}")
            if en == "scalar":
                W(f"rmax{t}")
                I(sc_s, eng.activation(rsm[:, :], ps4[:, 0:8], AF.Exp, bias=smalls[:, 4:5]), f"rexp{t}")
            if en == "vector":
                v = eng
                W(f"rexp{t}")
                rt(sc_v, v.reduce_sum(smalls[:, 5:6], rsm[:, :], axis=AX.X))
                rt(sc_v, v.reciprocal(smalls[:, 6:7], smalls[:, 5:6]))
                rt(sc_v, v.tensor_scalar(rsm[:, :], rsm[:, :], smalls[:, 6:7], None, ALU.mult))
                rt(sc_v, v.reduce_max(smalls[:, 7:8], rsm[:, :], axis=AX.X))
                rt(sc_v, v.tensor_scalar(rsm2[:, :], rsm[:, :], smalls[:, 7:8], None, ALU.is_equal))
                rt(sc_v, v.scalar_tensor_tensor(rsm3[:, :], rsm2[:, :], -1e30, rsm[:, :], ALU.mult, ALU.add))
                I(sc_v, v.reduce_max(smalls[:, 8:9], rsm3[:, :], axis=AX.X), f"rtop{t}")
            if en == "scalar":
                s = eng
                W(f"rtop{t}")
                s.activation(smalls[:, 9:10], smalls[:, 7:8], AF.Exp)
                I(sc_s, s.activation(smalls[:, 10:11], smalls[:, 8:9], AF.Exp), f"rexp2{t}")
            if en == "vector":
                v = eng
                W(f"rexp2{t}")
                rt(sc_v, v.tensor_tensor(smalls[:, 11:12], smalls[:, 9:10], smalls[:, 10:11], ALU.add))
                rt(sc_v, v.reciprocal(smalls[:, 12:13], smalls[:, 11:12]))
                rt(sc_v, v.tensor_scalar(rsm2[:, :], rsm2[:, :], smalls[:, 9:10], smalls[:, 12:13], ALU.mult, ALU.mult))
                rt(sc_v, v.tensor_scalar(rsm4[:, :], rsm3[:, :], smalls[:, 8:9], None, ALU.is_equal))
                rt(sc_v, v.tensor_scalar(rsm4[:, :], rsm4[:, :], smalls[:, 10:11], smalls[:, 12:13], ALU.mult, ALU.mult))
                I(sc_v, v.tensor_tensor(w_rout[:, t * E:t * E + E], rsm2[:, :], rsm4[:, :], ALU.add), f"wrout{t}")
            if en == "scalar":
                W(f"rope_q{t}")
                I(sc_s, eng.activation(sqb[:, :], qro[:, :], AF.Square), f"sqq{t}")
            if en == "vector":
                W(f"sqq{t}")
                I(sc_v, eng.reduce_sum(qn2t[:, :], sqb[:, :].rearrange("p (h f) -> p h f", h=H), axis=AX.X), f"qn2{t}")
            if en == "tensor":
                W(f"qn2{t}")
                W(led["ps1b"])
                I(sc_t, eng.transpose(ps1[0:16, 0:128], qn2t[:, :], identf[:, :]), f"qn2T{t}")
            if en == "vector":
                W(f"qn2T{t}")
                I(sc_v, eng.reduce_max(Mbuf[:, t:t + 1], ps1[0:16, 0:128], axis=AX.X), f"Mb{t}")
            led["ps1b"] = f"Mb{t}"
            if en == "scalar":
                W(f"rope_k{t}")
                W(f"qn2{t}")
                I(sc_s, eng.activation(sqb[:, :], kro[:, :], AF.Square), f"sqk{t}")
            if en == "vector":
                v = eng
                W(f"sqk{t}")
                rt(sc_v, v.reduce_sum(kn2t[:, :], sqb[:, :].rearrange("p (h f) -> p h f", h=H), axis=AX.X))
                I(sc_v, v.reduce_max(kmaxb[:, t:t + 1], kn2t[:, :], axis=AX.X), f"kn2{t}")
            for name, rob, tile, store in (("q", qro, qTt, qT_s), ("k", kro, kTt, kT_s)):
                for c in range(NC_):
                    bank = "ps0b" if c % 2 == 0 else "ps1b"
                    pd = ps0b if c % 2 == 0 else ps1b
                    if en == "tensor":
                        W(led[bank])
                        key = f"{name}Tm{t}_{c}"
                        I(sc_t, eng.matmul(pd[:, 0:128], rob[:, c * 128:c * 128 + 128], ident[:, :],
                                           is_transpose=True), key)
                        if c == NC_ - 1:
                            EV[f"{name}Tmm{t}"] = EV[key]
                    if en == "scalar":
                        W(f"{name}Tm{t}_{c}")
                        if c == 0 and t >= 1:
                            W(f"{name}st{t-1}")
                        I(sc_s, eng.copy(tile[:, c * 128:c * 128 + 128], pd[:, 0:128]), f"{name}Te{t}_{c}")
                    led[bank] = f"{name}Te{t}_{c}"
                if en == "scalar":
                    W(f"{name}Te{t}_{NC_-1}")
                    dma(eng, sems["sq_s"], store[t], tile[:, :], f"{name}st{t}")

        if en == "sync":
            for i_ in range(4):
                dma(eng, sems["sq_x"], out_h[i_ * 1024:(i_ + 1) * 1024, :],
                    x_h[i_ * 1024:(i_ + 1) * 1024, :], "xcopy" if i_ == 3 else None)

        if "p15" in FENCES:
            yield
        if en == "tensor":
            W(f"kn2{NT-1}")
            W(led["ps1b"])
            I(sc_t, eng.transpose(ps1[0:32, 0:128], kmaxb[:, :], identf[:, :]), "km1")
        if en == "vector":
            W("km1")
            I(sc_v, eng.reduce_max(kk1[:, :], ps1[0:32, 0:128], axis=AX.X), "km2")
        if en == "tensor":
            W("km2")
            I(sc_t, eng.transpose(ps1[0:1, 0:32], kk1[:, :], identf[0:32, 0:32]), "km3")
        if en == "vector":
            v = eng
            W("km3")
            I(sc_v, v.reduce_max(kk2[:, :], ps1[0:1, 0:32], axis=AX.X), "km4")
            W(f"Mb{NT-1}")
            I(sc_v, v.reduce_max(qn2blk[:, :], Mbuf[:, :].rearrange("p (b f) -> p b f", b=8), axis=AX.X), "qbm")
        if en == "tensor":
            W("km4")
            I(sc_t, eng.matmul(ps1[0:16, 0:1], ones_f[0:1, 0:16], kk2[0:1, :], start=True, stop=True), "km5")
        if en == "vector":
            v = eng
            W("km5")
            rt(sc_v, v.tensor_copy(maxk_col[:, :], ps1[0:16, 0:1]), "km6")
            W("qbm")
            I(sc_v, v.tensor_scalar(qn2blk[:, :], qn2blk[:, :], maxk_col[:, 0:1], None, ALU.mult), "prod")
        led["ps1b"] = "km6"
        if en == "scalar":
            s = eng
            W("prod")
            rt(sc_s, s.activation(negm16[:, :], qn2blk[:, :], AF.Ln))
            I(sc_s, s.activation(negm16[:, :], negm16[:, :], AF.Exp, scale=0.5), "negm_a")
        if en == "vector":
            W("negm_a")
            I(sc_v, eng.tensor_scalar(negm16[:, :], negm16[:, :], -1.01, None, ALU.mult), "negm")
        if en == "sync":
            sync = eng
            W("negm")
            v1 = dma(sync, sems["sq_x"], negm_d.rearrange("(h e) -> h e", h=16), negm16[:, :])
            sync.wait_ge(sems["sq_x"].h, v1[1])
            dma(sync, sems["sq_x"], negm_fl[0:1, :], negm_d.rearrange("(o f) -> o f", o=1), "negm_fl")
        if en == "tensor":
            W("negm_fl")
            W(led["ps1b"])
            I(sc_t, eng.matmul(ps1[:, 0:128], ones_f[0:1, :], negm_fl[0:1, :], start=True, stop=True), "biasT_mm")
        if en == "vector":
            W("biasT_mm")
            I(sc_v, eng.tensor_copy(biasT[:, :], ps1[:, 0:128]), "biasT")
        led["ps1b"] = "biasT"

        g_acc_ev = [None] * NT
        last_sil = [None, None]
        last_hm = [None, None]
        last_dn = None
        last_stg = {0: None, 1: None}
        last_gacc_stg = {0: None, 1: None}
        for e in range(E):
            if "exp" in FENCES:
                yield
            if en == "sync":
                sy = eng
                W(f"wgc{e}")
                if e == 0:
                    W("biasT")
                    W(f"kst{NT-1}")
                else:
                    W(f"gu_done{e-1}")
                dma(sy, sems["sq_x"], wgsb[:, :].rearrange("p (c f) -> p c f", c=NC_),
                    wg_b[e].rearrange("(c p) f -> p c f", p=128))
                dma(sy, sems["sq_x"], wusb[:, :].rearrange("p (c f) -> p c f", c=NC_),
                    wu_b[e].rearrange("(c p) f -> p c f", p=128), f"wgu{e}")
            if en == "scalar":
                W(f"wdc{e}")
                if e == 0:
                    W("biasT")
                    W(f"kst{NT-1}")
                else:
                    W(f"dn_done{e-1}")
                dma(eng, sems["sq_s"], wdsb[:, :].rearrange("p (f d) -> p f d", f=NFC),
                    wd_b[e].rearrange("(f p) d -> p f d", p=128), f"wd{e}")
            for grp in range(NGRP):
                t0 = grp * GRP
                for fc in range(NFC):
                    psG = ps4 if fc % 2 == 0 else psA
                    if en == "tensor":
                        te = eng
                        if fc == 0 and grp == 0:
                            W(f"wgu{e}")
                            if e == 0:
                                W(f"xnTe{NT-1}_{NC_-1}")
                                W(f"wrout{NT-1}")
                        W(last_sil[0])
                        for c in range(NC_):
                            te.matmul(psG[:, 0:512], wgsb[:, c * FF + fc * 128:c * FF + fc * 128 + 128],
                                      xnT[:, c * S + t0:c * S + t0 + GRP], start=(c == 0), stop=(c == NC_ - 1))
                        W(last_hm[0])
                        for c in range(NC_):
                            last = te.matmul(psG[:, 512:1024], wusb[:, c * FF + fc * 128:c * FF + fc * 128 + 128],
                                             xnT[:, c * S + t0:c * S + t0 + GRP], start=(c == 0), stop=(c == NC_ - 1))
                        key = f"gu{e}_{grp}_{fc}"
                        I(sc_t, last, key)
                        if grp == NGRP - 1 and fc == NFC - 1:
                            EV[f"gu_done{e}"] = EV[key]
                    if en == "scalar":
                        W(f"gu{e}_{grp}_{fc}")
                        if fc == 0 and grp == 0 and e == 0:
                            W("biasT")
                        I(sc_s, eng.activation(hsil[:, (fc % 2) * GRP:(fc % 2) * GRP + GRP], psG[:, 0:512],
                                               AF.Silu), f"sil{e}_{grp}_{fc}")
                    last_sil = [last_sil[1], f"sil{e}_{grp}_{fc}"]
                    if en == "vector":
                        W(f"sil{e}_{grp}_{fc}")
                        W(last_dn)
                        I(sc_v, eng.tensor_tensor(hT[:, fc * GRP:fc * GRP + GRP], psG[:, 512:1024],
                                                  hsil[:, (fc % 2) * GRP:(fc % 2) * GRP + GRP],
                                                  ALU.mult), f"hm{e}_{grp}_{fc}")
                    last_hm = [last_hm[1], f"hm{e}_{grp}_{fc}"]
                for ti in range(4):
                    tt = grp * 4 + ti
                    par = ti % 2
                    psD = ps3 if par == 0 else ps2
                    if en == "tensor":
                        te = eng
                        if ti == 0:
                            W(f"hm{e}_{grp}_{NFC-1}")
                            if grp == 0:
                                W(f"wd{e}")
                                if e == 0:
                                    W(f"rope_q{NT-1}")
                                    W(f"rope_k{NT-1}")
                        W(last_stg[par])
                        for fc in range(NFC):
                            for hf in range(2):
                                last = te.matmul(psD[:, hf * 512:hf * 512 + 512],
                                                 hT[:, fc * GRP + ti * 128:fc * GRP + ti * 128 + 128],
                                                 wdsb[:, fc * 1024 + hf * 512:fc * 1024 + hf * 512 + 512],
                                                 start=(fc == 0), stop=(fc == NFC - 1))
                        key = f"dnm{e}_{tt}"
                        I(sc_t, last, key)
                        if ti == 3:
                            EV[f"dn{e}_{grp}"] = EV[key]
                            if grp == NGRP - 1:
                                EV[f"dn_done{e}"] = EV[key]
                    if ti == 3:
                        last_dn = f"dn{e}_{grp}"
                    if en == "vector":
                        W(f"dnm{e}_{tt}")
                        W(last_gacc_stg[tt % 2])
                        I(sc_v, eng.tensor_scalar(stg[:, (tt % 2) * 1024:(tt % 2) * 1024 + 1024], psD[:, :],
                                                  w_rout[:, tt * E + e:tt * E + e + 1], None, ALU.mult),
                          f"stg{e}_{tt}")
                    last_stg[par] = f"stg{e}_{tt}"
                    if en == "gpsimd":
                        g = eng
                        W(f"stg{e}_{tt}")
                        W(g_acc_ev[tt])
                        kw = dict(accum_op=ALU.add) if e >= 1 else {}
                        dma(g, sems["sq_w"], g_s[tt * 128:tt * 128 + 128, :],
                            stg[:, (tt % 2) * 1024:(tt % 2) * 1024 + 1024], f"gacc{e}_{tt}", **kw)
                    g_acc_ev[tt] = f"gacc{e}_{tt}"
                    last_gacc_stg[tt % 2] = f"gacc{e}_{tt}"

        P2_T = f"dn_done{E-1}"
        P2_V = f"stg{E-1}_{NT-1}"
        P2_G = f"gacc{E-1}_{NT-1}"

        if en == "gpsimd":
            g = eng
            W(P2_T); W(P2_V)
            dma(g, sems["sq_w"], stg_wo[:, :].rearrange("p (c d) -> p c d", c=8),
                wo_h.rearrange("(c p) d -> p c d", p=128), "wo_st")
        if en == "sync":
            sync = eng
            W(f"gu_done{E-1}")
            for t in range(NT):
                dma(sync, sems["sq_x"], kTb[:, t * 1024:t * 1024 + 1024], kT_s[t],
                    "kT_ld" if t == NT - 1 else None)
        if en == "scalar":
            s = eng
            W(P2_T); W(P2_V); W(P2_G)
            for t in range(NT):
                dma(s, sems["sq_s"], vsb3[:, t * H * 65:(t + 1) * H * 65], v_s[t],
                    "v_ld" if t == NT - 1 else None)
        for k in range(64):
            ec, dc = k // 8, k % 8
            bank = "ps0b" if k % 2 == 0 else "ps1b"
            pd = ps0b if k % 2 == 0 else ps1b
            if en == "tensor":
                if k == 0:
                    W("wo_st")
                W(led[bank])
                I(sc_t, eng.matmul(pd[:, 0:128], stg_wo[:, ec * 1024 + dc * 128:ec * 1024 + dc * 128 + 128],
                                   ident[:, :], is_transpose=True), f"woTm{k}")
            if en == "scalar":
                W(f"woTm{k}")
                I(sc_s, eng.copy(woT[:, dc * 1024 + ec * 128:dc * 1024 + ec * 128 + 128], pd[:, 0:128]),
                  f"woTe{k}")
            led[bank] = f"woTe{k}"

        def emit_wo_tile(qbw, ti):
            tt = 4 * qbw + ti
            if en == "sync":
                if qbw == 0 and ti == 0:
                    W(P2_G)
                for pf in ([tt, tt + 1] if ti == 0 else ([tt + 1] if ti < 3 else [])):
                    if 4 * qbw <= pf < 4 * qbw + 4:
                        if pf >= 2:
                            W(f"ost{pf-2}")
                        dma(eng, sems["sq_x"], gb3[:, (pf % 2) * 1024:(pf % 2) * 1024 + 1024],
                            g_s[pf * 128:pf * 128 + 128, :], f"g_ld{pf}")
            if en == "scalar":
                W(f"g_ld{tt}")
                if tt >= 1:
                    W(f"comb{tt-1}")
                I(sc_s, eng.activation(gb3s[:, :], gb3[:, (tt % 2) * 1024:(tt % 2) * 1024 + 1024],
                                       AF.Silu), f"gsil{tt}")
            if en == "tensor":
                te = eng
                W(f"ont{qbw}_{H-1}")
                W(buf_free[2])
                if tt == 0:
                    W("woTe63")
                if tt >= 1:
                    W(f"comb{tt-1}")
                for dc in range(NC_):
                    for hf in range(2):
                        last = te.matmul(ps4[:, hf * 512:hf * 512 + 512],
                                         ontT[:, (qbw % 2) * 4096 + dc * 512 + ti * 128:(qbw % 2) * 4096 + dc * 512 + ti * 128 + 128],
                                         woT[:, dc * 1024 + hf * 512:dc * 1024 + hf * 512 + 512],
                                         start=(dc == 0), stop=(dc == NC_ - 1))
                I(sc_t, last, f"womm{tt}")
            if en == "vector":
                v = eng
                W(f"womm{tt}")
                W(f"gsil{tt}")
                if tt >= 1:
                    W(f"ost{tt-1}")
                I(sc_v, v.tensor_tensor(gb3[:, (tt % 2) * 1024:(tt % 2) * 1024 + 1024],
                                        ps4[:, :], gb3s[:, :], ALU.mult), f"comb{tt}")
            if en == "gpsimd":
                g = eng
                W(f"comb{tt}")
                if tt == 0:
                    W("xcopy")
                dma(g, sems["sq_w"], out_h[tt * 128:tt * 128 + 128, :],
                    gb3[:, (tt % 2) * 1024:(tt % 2) * 1024 + 1024], f"ost{tt}", accum_op=ALU.add)
            buf_free[2] = f"comb{tt}"

        pair_i = 0
        buf_free = {0: None, 1: None, 2: None}
        pend_rd = [None]

        def emit_rdbcm(qbr, hr_):
            if en == "tensor":
                W(f"rd{qbr}_{hr_}")
                I(sc_t, eng.matmul(ps3.ap()[64:128, (hr_ % 2) * 512:(hr_ % 2) * 512 + 512],
                                   ones_f[0:1, 0:64], rdrow[0:1, :], start=True, stop=True),
                  f"rdbcm{qbr}_{hr_}")
            if en == "vector":
                v = eng
                W(f"rdbcm{qbr}_{hr_}")
                if hr_ == 0 and qbr >= 2:
                    W(f"womm{4*(qbr-1)-1}")
                rt(sc_v, v.tensor_copy(rdbc[:, :], ps3.ap()[64:128, (hr_ % 2) * 512:(hr_ % 2) * 512 + 512]),
                   f"rdbce{qbr}_{hr_}")
                I(sc_v, v.tensor_tensor(ontT[(hr_ % 2) * 64:(hr_ % 2) * 64 + 64,
                                             (qbr % 2) * 4096 + (hr_ // 2) * 512:(qbr % 2) * 4096 + (hr_ // 2) * 512 + 512],
                                        ps3[0:64, (hr_ % 2) * 512:(hr_ % 2) * 512 + 512],
                                        rdbc[0:64, :], ALU.mult), f"ont{qbr}_{hr_}")

        for qb in range(NQB):
            if "qb" in FENCES:
                yield
            if en == "sync":
                sync = eng
                if qb == 0:
                    W(P2_T)
                else:
                    W(f"smmP{pair_i - 1}")
                dma(sync, sems["sq_x"], qblk[:, :].rearrange("p (t c) -> p t c", t=4),
                    qT_s[4 * qb:4 * qb + 4].transpose([1, 0, 2]), f"q_ld{qb}")
            npair = 2 * qb + 2
            for h in range(H):
                if qb >= 1 and 1 <= h <= 4:
                    emit_wo_tile(qb - 1, h - 1)
                hp, hr = h // 2, h % 2
                cps0 = pair_i
                for idx in range(npair + 2):
                    if idx == 1 and pend_rd[0] is not None:
                        qbr, hr_ = pend_rd[0]
                        pend_rd[0] = None
                        emit_rdbcm(qbr, hr_)
                    if idx < npair:
                        pp = idx
                        cp = cps0 + pp
                        b = cp % 3
                        c0, c1_ = 2 * pp, 2 * pp + 1
                        psS = [psA.ap(), ps2.ap(), ps4.ap()][b]
                        pth = Pt[:, (cp % 2) * 1024:(cp % 2) * 1024 + 1024]
                        if en == "tensor":
                            te = eng
                            if h == 0 and pp == 0:
                                W("kT_ld")
                                W(f"q_ld{qb}")
                                if qb == 0:
                                    W("biasT")
                                    W(P2_V)
                            W(buf_free[b])
                            rq = qblk[hr * 64:hr * 64 + 64, :].rearrange("p (t c2) -> p t c2", t=4)[:, :, hp * 128:hp * 128 + 128]
                            te.matmul(psS[:, 0:512], kTb[hr * 64:hr * 64 + 64, c0 * 1024 + hp * 128:c0 * 1024 + hp * 128 + 128],
                                      rq, start=True, stop=True)
                            I(sc_t, te.matmul(psS[:, 512:1024], kTb[hr * 64:hr * 64 + 64, c1_ * 1024 + hp * 128:c1_ * 1024 + hp * 128 + 128],
                                              rq, start=True, stop=True), f"smmP{cp}")
                        if en == "scalar":
                            W(f"smmP{cp}")
                            if cp >= 2:
                                W(f"avP{cp-2}")
                            I(sc_s, eng.activation(pth, psS[:, :], AF.Exp,
                                                   bias=biasT[:, h * 8 + qb:h * 8 + qb + 1]), f"expP{cp}")
                        buf_free[b] = f"expP{cp}"
                        if c0 >= 4 * qb and en == "vector":
                            W(f"expP{cp}")
                            tloc = c0 - 4 * qb
                            I(sc_v, eng.tensor_tensor(pth, pth, masks[:, tloc * 512:tloc * 512 + 1024], ALU.mult),
                              f"mskP{cp}")
                    if idx >= 2:
                        pp = idx - 2
                        cp = cps0 + pp
                        b = cp % 3
                        c0, c1_ = 2 * pp, 2 * pp + 1
                        boundary = c0 >= 4 * qb
                        pth = Pt[:, (cp % 2) * 1024:(cp % 2) * 1024 + 1024]
                        if en == "tensor":
                            te = eng
                            W(f"mskP{cp}" if boundary else f"expP{cp}")
                            if pp == 0:
                                if h >= 2:
                                    W(f"ont{qb}_{h-2}")
                                elif qb >= 1:
                                    W(f"ont{qb-1}_{H-2+h}")
                                if h == 0 and qb == 0:
                                    W("v_ld")
                            psV = ps3.ap()[0:65, (h % 2) * 512:(h % 2) * 512 + 512]
                            te.matmul(psV, vsb3[:, c0 * H * 65 + h * 65:c0 * H * 65 + h * 65 + 65],
                                      pth[:, 0:512], start=(c0 == 0), stop=False)
                            key = f"avP{cp}"
                            I(sc_t, te.matmul(psV, vsb3[:, c1_ * H * 65 + h * 65:c1_ * H * 65 + h * 65 + 65],
                                              pth[:, 512:1024], start=False, stop=(c1_ == 4 * qb + 3)), key)
                            if c1_ == 4 * qb + 3:
                                EV[f"av{qb}_{h}"] = EV[key]
                                if h == H - 1:
                                    EV[f"sc_last{qb}"] = EV[key]
                pair_i += npair
                if en == "scalar":
                    s = eng
                    W(f"av{qb}_{h}")
                    psF = ps3.ap()
                    rt(sc_s, s.activation(lnrow[0:1, :], psF[64:65, (h % 2) * 512:(h % 2) * 512 + 512], AF.Ln))
                    I(sc_s, s.activation(rdrow[0:1, :], lnrow[0:1, :], AF.Exp, scale=-1.0), f"rd{qb}_{h}")
                if h == H - 1:
                    emit_rdbcm(qb, h)
                else:
                    pend_rd[0] = (qb, h)
            if qb == NQB - 1:
                for ti in range(4):
                    emit_wo_tile(qb, ti)



        if en == "scalar":
            EV["tot_sq_s"] = ("sq_s", sems["sq_s"].n)
        if en == "gpsimd":
            EV["tot_sq_w"] = ("sq_w", sems["sq_w"].n)
        if en == "sync":
            eng.wait_ge(sems["sq_x"].h, sems["sq_x"].n)
            W("tot_sq_s")
            W("tot_sq_w")

    n_fence = 1 + (1 if "p0" in FENCES else 0) + (NT if "tile" in FENCES else 0) \
        + (1 if "p15" in FENCES else 0) + (E if "exp" in FENCES else 0) + (NQB if "qb" in FENCES else 0)
    ENG = ("sync", "gpsimd", "scalar", "vector", "tensor")
    holders = {en: [None] for en in ENG}
    gens = {en: walk(en, holders[en]) for en in ENG}
    for en in ENG:
        next(gens[en])
    with nc.Block() as blk:
        for seg in range(n_fence):
            for en in ENG:
                def run(engobj, en=en):
                    holders[en][0] = engobj
                    try:
                        next(gens[en])
                    except StopIteration:
                        pass
                getattr(blk, en)(run)
    return nc, EV


def build(dbg=()):
    global EVp
    EVp = {}
    _, ev1 = build_once(dbg, EVp, final=False)
    EVp = ev1
    nc, ev2 = build_once(dbg, EVp, final=True)
    assert ev1 == ev2, "event mismatch between passes"
    return nc


EVp = {}
_NC_CACHE = {}


def kernel(**inputs):
    if "nc" not in _NC_CACHE:
        _NC_CACHE["nc"] = build()
    nc = _NC_CACHE["nc"]
    x = np.ascontiguousarray(inputs["L0_attn_x"])
    base = {
        "wq": np.ascontiguousarray(inputs["L0_attn_Wq"]),
        "wk": np.ascontiguousarray(inputs["L0_attn_Wk"]),
        "wv": np.ascontiguousarray(inputs["L0_attn_Wv"]),
        "wo": np.ascontiguousarray(inputs["L0_attn_Wo"]),
        "wr": np.ascontiguousarray(inputs["L0_ffn_kernel_router_DE"]),
        "wg": np.ascontiguousarray(inputs["L0_ffn_kernel_gating_EDF"]),
        "wu": np.ascontiguousarray(inputs["L0_ffn_kernel_up_EDF"]),
        "wd": np.ascontiguousarray(inputs["L0_ffn_kernel_down_EFD"]),
        "invf": np.ascontiguousarray(np.broadcast_to(
            (10000.0 ** (-np.arange(0, 64, 2, dtype=np.float64) / 64.0)).astype(np.float32)[None, :],
            (128, 32))),
    }
    in_maps = [dict(base, x=np.ascontiguousarray(x[b])) for b in range(8)]
    res = run_bass_kernel_spmd(nc, in_maps, core_ids=list(range(8)))
    return np.stack([res.results[b]["out"] for b in range(8)], axis=0)
